# version 14 (final)
# baseline (speedup 1.0000x reference)
import contextlib
import numpy as np
import concourse.bass as bass
import concourse.mybir as mybir
from concourse.bass_utils import run_bass_kernel_spmd

F32 = mybir.dt.float32
BF16 = mybir.dt.bfloat16
AF = mybir.ActivationFunctionType
ALU = mybir.AluOpType
AX = mybir.AxisListType.X

T = 2048
D = 1024
C8 = 8
TT = 512
NT = 4
TC = 128
NTC = 16
NF = 16
NEXP = 8
EPS_LN = 1e-5
EPS_GN = 0.00064
EPS_RMS = 1e-6
CTX_SCALE = float(np.sqrt(3.0 / 64.0))

V_XW, V_XK, V_XV, V_XA, V_XR, V_XG, V_MUR, V_MUK, V_BOUT, V_GNW, V_GNB, V_NW, V_EPSLN, V_EPSGN, V_EPSRMS = range(15)
NVEC = 16
PROJ = [
    ("w", V_XW, False, AF.Sigmoid, -1.0, False, "d_decay"),
    ("k", V_XK, False, AF.Tanh, 1.0, False, "d_ktanh"),
    ("v", V_XV, False, AF.Identity, 1.0, False, "d_v"),
    ("a", V_XA, False, AF.Sigmoid, 1.0, False, "d_a"),
    ("r", V_XR, False, AF.Sigmoid, 1.0, False, "d_r"),
    ("g", V_XG, False, AF.Sigmoid, 1.0, True, "d_g"),
    ("fr", V_MUR, True, AF.Sigmoid, 1.0, False, "d_rf"),
    ("fk", V_MUK, True, AF.Relu, 1.0, False, "d_kact"),
]

FENCE_SEMS = ("mm", "ve", "se", "st", "ax", "gd", "po")
DEBUG_L0 = False


class Env:
    def __init__(self, nc, ctx, block):
        self.nc = nc
        self.ctx = ctx
        self.block = block
        self._sems = {}
        self.cnt = {}

    def s(self, name):
        if name not in self._sems:
            self._sems[name] = self.ctx.enter_context(self.nc.semaphore(name))
            self.cnt[name] = 0
        return self._sems[name]

    def inc(self, ins, name, k=1):
        ins.then_inc(self.s(name), k)
        self.cnt[name] += k
        return self.cnt[name]

    def w(self, eng, name, val):
        if val > 0:
            eng.wait_ge(self.s(name), val)

    def bar(self, eng, ins, name="ve"):
        c = self.inc(ins, name)
        eng.wait_ge(self.s(name), c)

    def snap(self):
        return dict(self.cnt)

    def fence(self, eng, snap):
        for n in FENCE_SEMS:
            v = snap.get(n, 0)
            if v > 0:
                eng.wait_ge(self.s(n), v)


_UNIQ = [0]


def uniq(name):
    _UNIQ[0] += 1
    return f"{name}_{_UNIQ[0]}"


def sb(env, name, shape, dtype):
    return env.ctx.enter_context(env.nc.sbuf_tensor(name, shape, dtype))


def ps(env, name, shape, dtype=F32):
    return env.ctx.enter_context(env.nc.psum_tensor(name, shape, dtype))


def emit_setup(env, io):
    g = {}
    g["vt"] = sb(env, "sb_vt", [128, C8, NVEC], F32)
    g["cm2"] = sb(env, "cm2", [128, C8, 8], F32)
    g["idn"] = sb(env, "sb_idn", [128, 128], F32)
    g["gmask"] = sb(env, "sb_gmask", [128, 2], F32)
    g["gsel"] = sb(env, "sb_gsel", [2, 128], F32)
    g["oneh"] = sb(env, "sb_oneh", [8, 8 * 128], BF16)
    g["onesk"] = sb(env, "sb_onesk", [128, 1], F32)
    g["onebc"] = sb(env, "sb_onebc", [1, 128], F32)

    @env.block.gpsimd
    def _(eng):
        env.inc(eng.dma_start(out=g["vt"][:], in_=io["vecs"].rearrange("(c p) v -> p c v", p=128)), "gd", 16)
        env.inc(eng.dma_start(out=g["idn"][:], in_=io["idn"][:, :]), "gd", 16)
        env.inc(eng.dma_start(out=g["gmask"][:], in_=io["gmask"][:, :]), "gd", 16)
        env.inc(eng.dma_start(out=g["gsel"][:], in_=io["gsel"][:, :]), "gd", 16)
        env.inc(eng.dma_start(out=g["oneh"][:], in_=io["oneh"][:, :]), "gd", 16)
        env.inc(eng.dma_start(out=g["onesk"][:], in_=io["onesk"][:, :]), "gd", 16)
        env.inc(eng.dma_start(out=g["onebc"][:], in_=io["onebc"][:, :]), "gd", 16)
    gd_done = env.cnt["gd"]

    @env.block.vector
    def _(eng):
        env.w(eng, "gd", gd_done)
        env.inc(eng.tensor_scalar(g["cm2"][:], g["vt"][:, :, 0:8], -1.0, 1.0, ALU.mult, ALU.add), "ve")
    return g


def emit_tp_in(env, io, g, x_fm):
    snap = env.snap()
    with contextlib.ExitStack() as pc:
        xtm = [pc.enter_context(env.nc.sbuf_tensor(uniq(f"ig_xtm{i}"), [128, D], F32)) for i in range(2)]
        xstg = [pc.enter_context(env.nc.sbuf_tensor(uniq(f"ig_xstg{i}"), [128, C8, TC], F32)) for i in range(2)]
        pt = [pc.enter_context(env.nc.psum_tensor(uniq(f"ig_pt{i}"), [128, TC], F32)) for i in range(4)]

        gd0, mm0, ve0, se0, st0 = (snap.get(k, 0) for k in ("gd", "mm", "ve", "se", "st"))
        ld_at = [gd0 + 16 * (i + 1) for i in range(NTC)]
        mm_tp = [mm0 + 8 * (i + 1) for i in range(NTC)]
        ve_ev = [ve0 + 8 * (i + 1) for i in range(NTC)]
        se_ev = [se0 + 8 * (i + 1) for i in range(NTC)]
        st_at = [st0 + 16 * (i + 1) for i in range(NTC)]

        @env.block.gpsimd
        def _(eng):
            env.fence(eng, snap)
            for tc in range(NTC):
                if tc >= 2:
                    env.w(eng, "mm", mm_tp[tc - 2])
                env.inc(eng.dma_start(out=xtm[tc % 2][:], in_=io["x"][tc * TC:(tc + 1) * TC, :]), "gd", 16)

        @env.block.tensor
        def _(eng):
            env.fence(eng, snap)
            for tc in range(NTC):
                env.w(eng, "gd", ld_at[tc])
                for cb in range(C8):
                    idx = tc * 8 + cb
                    if idx >= 4:
                        pv_tc, pv_cb = divmod(idx - 4, 8)
                        if pv_cb % 2 == 0:
                            env.w(eng, "ve", ve0 + pv_tc * 8 + (pv_cb // 2) * 2 + 1)
                        else:
                            env.w(eng, "se", se0 + pv_tc * 8 + (pv_cb // 2) * 2 + 1)
                    ins = eng.transpose(pt[cb % 4][:], xtm[tc % 2][:, cb * 128:(cb + 1) * 128], g["idn"][:])
                    env.inc(ins, "mm")

        @env.block.vector
        def _(eng):
            env.fence(eng, snap)
            for tc in range(NTC):
                if tc >= 2:
                    env.w(eng, "st", st_at[tc - 2])
                for cb in range(0, C8, 2):
                    env.w(eng, "mm", mm0 + tc * 8 + cb + 1)
                    eng.tensor_copy(x_fm[:, cb * T + tc * TC: cb * T + (tc + 1) * TC], pt[cb % 4][:])
                    env.inc(eng.tensor_copy(xstg[tc % 2][:, cb, :], pt[cb % 4][:]), "ve", 2)

        @env.block.scalar
        def _(eng):
            env.fence(eng, snap)
            for tc in range(NTC):
                if tc >= 2:
                    env.w(eng, "st", st_at[tc - 2])
                for cb in range(1, C8, 2):
                    env.w(eng, "mm", mm0 + tc * 8 + cb + 1)
                    eng.activation(x_fm[:, cb * T + tc * TC: cb * T + (tc + 1) * TC], pt[cb % 4][:], AF.Identity)
                    env.inc(eng.activation(xstg[tc % 2][:, cb, :], pt[cb % 4][:], AF.Identity), "se", 2)

        @env.block.sync
        def _(eng):
            env.fence(eng, snap)
            for tc in range(NTC):
                env.w(eng, "ve", ve_ev[tc])
                env.w(eng, "se", se_ev[tc])
                ins = eng.dma_start(
                    out=io["d_x1"].rearrange("(c p) n -> p c n", p=128)[:, :, tc * TC:(tc + 1) * TC],
                    in_=xstg[tc % 2][:])
                env.inc(ins, "st", 16)


def emit_ln_fm(env, g, xsb, eps_col, rms=False, nw=False):
    snap = env.snap()
    ve_done = []
    with contextlib.ExitStack() as pc:
        psA = [pc.enter_context(env.nc.psum_tensor(uniq(f"ln_a{i}"), [128, TT], F32)) for i in range(NT)]
        psB = [pc.enter_context(env.nc.psum_tensor(uniq(f"ln_b{i}"), [128, TT], F32)) for i in range(NT)]
        sqb = [pc.enter_context(env.nc.sbuf_tensor(uniq(f"ln_sq{i}"), [128, T], F32)) for i in range(2)]
        rows = pc.enter_context(env.nc.sbuf_tensor(uniq("ln_rows"), [1, 6 * T], F32))
        mbc = pc.enter_context(env.nc.sbuf_tensor(uniq("ln_mbc"), [128, T], F32))
        rbc = pc.enter_context(env.nc.sbuf_tensor(uniq("ln_rbc"), [128, T], F32))

        mm0, ve0, se0 = (snap.get(k, 0) for k in ("mm", "ve", "se"))
        mm_per_c = NT * (2 if not rms else 1)
        se_sq = [se0 + c + 1 for c in range(C8)]
        mm_stats = mm0 + C8 * mm_per_c
        n_ev = NT * (2 if not rms else 1)
        ve_rows = ve0 + n_ev + (4 if not rms else 1)
        mm_bc = mm_stats + (2 * NT if not rms else NT)

        @env.block.scalar
        def _(eng):
            env.fence(eng, snap)
            for c in range(C8):
                if c >= 2:
                    env.w(eng, "mm", mm0 + (c - 1) * mm_per_c)
                env.inc(eng.activation(sqb[c % 2][:], xsb[:, c * T:(c + 1) * T], AF.Square), "se")
            env.w(eng, "ve", ve_rows)
            eng.activation(rows[0:1, 4 * T:5 * T], rows[0:1, 3 * T:4 * T], AF.Ln,
                           bias=g["vt"][0:1, 0, eps_col:eps_col + 1])
            env.inc(eng.activation(rows[0:1, 5 * T:6 * T], rows[0:1, 4 * T:5 * T], AF.Exp,
                                   scale=-0.5), "se", 2)

        se_rstd = se0 + C8 + 2

        @env.block.tensor
        def _(eng):
            env.fence(eng, snap)
            for c in range(C8):
                env.w(eng, "se", se_sq[c])
                for t in range(NT):
                    if not rms:
                        eng.matmul(psA[t][0:1, :], g["onesk"][:], xsb[:, c * T + t * TT: c * T + (t + 1) * TT],
                                   start=(c == 0), stop=(c == C8 - 1))
                        env.inc(eng.matmul(psB[t][0:1, :], g["onesk"][:], sqb[c % 2][:, t * TT:(t + 1) * TT],
                                           start=(c == 0), stop=(c == C8 - 1)), "mm", 2)
                    else:
                        env.inc(eng.matmul(psB[t][0:1, :], g["onesk"][:], sqb[c % 2][:, t * TT:(t + 1) * TT],
                                           start=(c == 0), stop=(c == C8 - 1)), "mm")
            env.w(eng, "ve", ve_rows)
            env.w(eng, "se", se_rstd)
            for t in range(NT):
                if not rms:
                    eng.matmul(psA[t][:], g["onebc"][:], rows[0:1, 2 * T + t * TT: 2 * T + (t + 1) * TT],
                               start=True, stop=True)
                env.inc(eng.matmul(psB[t][:], g["onebc"][:], rows[0:1, 5 * T + t * TT: 5 * T + (t + 1) * TT],
                                   start=True, stop=True), "mm", 2 if not rms else 1)
            assert env.cnt["mm"] == mm_bc

        @env.block.vector
        def _(eng):
            env.fence(eng, snap)
            env.w(eng, "mm", mm_stats)
            for t in range(NT):
                if not rms:
                    eng.tensor_copy(rows[0:1, t * TT:(t + 1) * TT], psA[t][0:1, :])
                    env.inc(eng.tensor_copy(rows[0:1, T + t * TT: T + (t + 1) * TT], psB[t][0:1, :]), "ve", 2)
                else:
                    env.inc(eng.tensor_copy(rows[0:1, T + t * TT: T + (t + 1) * TT], psB[t][0:1, :]), "ve")
            if not rms:
                eng.tensor_scalar_mul(rows[0:1, 2 * T:3 * T], rows[0:1, 0:T], 1.0 / D)
                env.inc(eng.tensor_scalar_mul(rows[0:1, 3 * T:4 * T], rows[0:1, T:2 * T], 1.0 / D), "ve", 2)
                eng.tensor_mul(rows[0:1, 4 * T:5 * T], rows[0:1, 2 * T:3 * T], rows[0:1, 2 * T:3 * T])
                env.inc(eng.tensor_sub(rows[0:1, 3 * T:4 * T], rows[0:1, 3 * T:4 * T], rows[0:1, 4 * T:5 * T]), "ve", 2)
            else:
                env.inc(eng.tensor_scalar_mul(rows[0:1, 3 * T:4 * T], rows[0:1, T:2 * T], 1.0 / D), "ve")
            assert env.cnt["ve"] == ve_rows
            env.w(eng, "mm", mm_bc)
            for t in range(NT):
                if not rms:
                    eng.tensor_copy(mbc[:, t * TT:(t + 1) * TT], psA[t][:])
                env.inc(eng.tensor_copy(rbc[:, t * TT:(t + 1) * TT], psB[t][:]), "ve")
            for c in range(C8):
                xc = xsb[:, c * T:(c + 1) * T]
                if not rms:
                    eng.tensor_sub(xc, xc, mbc[:])
                    env.inc(eng.tensor_mul(xc, xc, rbc[:]), "ve")
                else:
                    if nw:
                        eng.tensor_mul(xc, xc, rbc[:])
                        env.inc(eng.tensor_scalar(xc, xc, g["vt"][:, c, V_NW:V_NW + 1], None, ALU.mult), "ve")
                    else:
                        env.inc(eng.tensor_mul(xc, xc, rbc[:]), "ve")
                ve_done.append(env.cnt["ve"])
    return ve_done


def emit_l0_proj(env, io, g, xn_fm, wb=None, wd_pre=None):
    snap = env.snap()
    with contextlib.ExitStack() as pc:
        E = pc.enter_context
        d16 = E(env.nc.sbuf_tensor(uniq("pj_d16"), [128, C8 * T], BF16))
        m16 = [E(env.nc.sbuf_tensor(uniq(f"pj_m{i}"), [128, C8, TT], BF16)) for i in range(2)]
        if wb is None:
            wb = [E(env.nc.sbuf_tensor(uniq(f"pj_w{i}"), [128, C8 * D], BF16)) for i in range(2)]
            wd_pre = []
        stg = [E(env.nc.sbuf_tensor(uniq(f"pj_stg{i}"), [128, C8, TT], F32)) for i in range(2)]
        stg16 = [E(env.nc.sbuf_tensor(uniq(f"pj_stg16_{i}"), [128, C8, TT], BF16)) for i in range(2)]
        pmm = [E(env.nc.psum_tensor(uniq(f"pj_p{i}"), [128, TT], F32)) for i in range(2)]

        NP = len(PROJ)
        mm0, ve0, se0, st0, gd0 = (snap.get(k, 0) for k in ("mm", "ve", "se", "st", "gd"))

        npre = len(wd_pre)
        wd_at = list(wd_pre) + [gd0 + 16 * 8 * (p + 1 - npre) for p in range(npre, NP)]
        ve_m, mm_g, mm_pt, se_e, ve_x, st_t = {}, {}, {}, {}, {}, {}
        ve = ve0 + C8 * 2
        mm, se, st = mm0, se0, st0
        for p in range(NP):
            for t in range(NT):
                ve += C8
                ve_m[(p, t)] = ve
                for o in range(C8):
                    mm += 1
                    mm_g[(p, t, o)] = mm
                    se += 1
                    se_e[(p, t, o)] = se
                    if p == 7:
                        ve += 1
                mm_pt[(p, t)] = mm
                if p == 7:
                    ve_x[(p, t)] = ve
                st += 16
                st_t[(p, t)] = st

        @env.block.gpsimd
        def _(eng):
            env.fence(eng, snap)
            for p in range(npre, NP):
                if p >= 2:
                    env.w(eng, "mm", mm_pt[(p - 2, NT - 1)])
                for c in range(C8):
                    env.inc(eng.dma_start(out=wb[p % 2][:, c * D:(c + 1) * D],
                                          in_=io["wp"][p, c * 128:(c + 1) * 128, :]), "gd", 16)
            assert env.cnt["gd"] == wd_at[-1]

        @env.block.vector
        def _(eng):
            env.fence(eng, snap)
            for c in range(C8):
                eng.tensor_sub(d16[:, c * T + 1:(c + 1) * T],
                               xn_fm[:, c * T:(c + 1) * T - 1],
                               xn_fm[:, c * T + 1:(c + 1) * T])
                env.inc(eng.tensor_scalar_mul(d16[:, c * T:c * T + 1], xn_fm[:, c * T:c * T + 1], -1.0), "ve", 2)
            for p in range(NP):
                _, vcol, one_minus, _, _, _, _ = PROJ[p]
                for t in range(NT):
                    pu = (p, t - 2) if t >= 2 else ((p - 1, t + 2) if p >= 1 else None)
                    if pu is not None:
                        env.w(eng, "mm", mm_pt[pu])
                    for c in range(C8):
                        coef = g["cm2"][:, c, vcol:vcol + 1] if one_minus else g["vt"][:, c, vcol:vcol + 1]
                        env.inc(eng.scalar_tensor_tensor(
                            m16[t % 2][:, c, :],
                            d16[:, c * T + t * TT: c * T + (t + 1) * TT],
                            coef,
                            xn_fm[:, c * T + t * TT: c * T + (t + 1) * TT],
                            ALU.mult, ALU.add), "ve")
                    assert env.cnt["ve"] == ve_m[(p, t)]
                    if p == 7:
                        for o in range(C8):
                            env.w(eng, "se", se_e[(p, t, o)])
                            env.inc(eng.tensor_mul(stg16[t % 2][:, o, :], stg[t % 2][:, o, :], stg[t % 2][:, o, :]), "ve")
                        assert env.cnt["ve"] == ve_x[(p, t)]

        @env.block.tensor
        def _(eng):
            env.fence(eng, snap)
            for p in range(NP):
                env.w(eng, "gd", wd_at[p])
                for t in range(NT):
                    env.w(eng, "ve", ve_m[(p, t)])
                    for o in range(C8):
                        gidx = (p * NT + t) * C8 + o
                        if gidx >= 2:
                            pv = gidx - 2
                            pp, rr = divmod(pv, NT * C8)
                            tt_, oo = divmod(rr, C8)
                            env.w(eng, "se", se_e[(pp, tt_, oo)])
                        for c in range(C8):
                            ins = eng.matmul(pmm[gidx % 2][:],
                                             wb[p % 2][:, c * D + o * 128: c * D + (o + 1) * 128],
                                             m16[t % 2][:, c, :],
                                             start=(c == 0), stop=(c == C8 - 1))
                        assert env.inc(ins, "mm") == mm_g[(p, t, o)]

        @env.block.scalar
        def _(eng):
            env.fence(eng, snap)
            for p in range(NP):
                _, _, _, fn, scale, bf16_out, _ = PROJ[p]
                for t in range(NT):
                    if (p * NT + t) >= 2:
                        pv = (p * NT + t) - 2
                        env.w(eng, "st", st_t[(pv // NT, pv % NT)])
                    for o in range(C8):
                        gidx = (p * NT + t) * C8 + o
                        env.w(eng, "mm", mm_g[(p, t, o)])
                        dst = stg16[t % 2][:, o, :] if bf16_out else stg[t % 2][:, o, :]
                        ins = eng.activation(dst, pmm[gidx % 2][:], fn, scale=scale)
                        assert env.inc(ins, "se") == se_e[(p, t, o)]

        @env.block.sync
        def _(eng):
            env.fence(eng, snap)
            for p in range(NP):
                bf16_out = PROJ[p][5]
                use16 = bf16_out or p == 7
                for t in range(NT):
                    if p == 7:
                        env.w(eng, "ve", ve_x[(p, t)])
                    else:
                        env.w(eng, "se", se_e[(p, t, C8 - 1)])
                    src = stg16[t % 2][:] if use16 else stg[t % 2][:]
                    ins = eng.dma_start(
                        out=io[PROJ[p][6]].rearrange("(o p) n -> p o n", p=128)[:, :, t * TT:(t + 1) * TT],
                        in_=src)
                    assert env.inc(ins, "st", 16) == st_t[(p, t)]


def emit_l0_scan(env, io, g, mout, wo_sb=None):
    snap = env.snap()
    with contextlib.ExitStack() as pc:
        E = pc.enter_context
        db = [E(env.nc.sbuf_tensor(uniq(f"sc_d{i}"), [128, T], F32)) for i in range(2)]
        ab = [E(env.nc.sbuf_tensor(uniq(f"sc_a{i}"), [128, T], F32)) for i in range(2)]
        kb = E(env.nc.sbuf_tensor(uniq("sc_k"), [128, T], F32))
        vb = E(env.nc.sbuf_tensor(uniq("sc_v"), [128, T], F32))
        rb = [E(env.nc.sbuf_tensor(uniq(f"sc_r{i}"), [128, T], F32)) for i in range(2)]
        gb = [E(env.nc.sbuf_tensor(uniq(f"sc_g{i}"), [128, T], BF16)) for i in range(2)]
        mx = [E(env.nc.sbuf_tensor(uniq(f"sc_m{i}"), [128, T], F32)) for i in range(2)]
        sqb = E(env.nc.sbuf_tensor(uniq("sc_q"), [128, T], F32))
        rows = E(env.nc.sbuf_tensor(uniq("sc_rows"), [2, 3 * T], F32))
        mbc = E(env.nc.sbuf_tensor(uniq("sc_mbc"), [128, T], F32))
        rbc = E(env.nc.sbuf_tensor(uniq("sc_rbc"), [128, T], F32))
        ps8 = [E(env.nc.psum_tensor(uniq(f"sc_ps{i}"), [128, TT], F32)) for i in range(8)]

        st0, ve0, se0, mm0, po0, gd0 = (snap.get(k, 0) for k in ("st", "ve", "se", "mm", "po", "gd"))
        st_ld = [st0 + 96 * (c + 1) for c in range(C8)]
        ve_u = [ve0 + 4 * c + 2 for c in range(C8)]
        ve_mx = [ve0 + 4 * c + 4 for c in range(C8)]
        po_var = [po0 + 5 * c + 1 for c in range(C8)]
        po_gn = [po0 + 5 * c + 5 for c in range(C8)]

        se_bc, se_sq, se_ev, se_msq, se_rstd = {}, {}, {}, {}, {}
        mm_bc, mm_st = {}, {}
        se = se0
        mm = mm0
        for s in range(10):
            if 2 <= s <= C8 + 1:
                se += 8
                se_bc[s - 2] = se
            if s < C8:
                se += 1
                se_sq[s] = se
            if 1 <= s <= C8:
                c = s - 1
                se += 8
                se_ev[c] = se
                se += 1
                se_msq[c] = se
                se += 2
                se_rstd[c] = se
            if 1 <= s <= C8:
                mm += 8
                mm_bc[s - 1] = mm
            if s < C8:
                mm += 8
                mm_st[s] = mm

        @env.block.sync
        def _(eng):
            env.fence(eng, snap)
            for c in range(C8):
                r = slice(c * 128, (c + 1) * 128)
                if c >= 2:
                    env.w(eng, "ve", ve_mx[c - 2])
                for nm, buf in (("d_decay", db), ("d_a", ab)):
                    env.inc(eng.dma_start(out=buf[c % 2][:], in_=io[nm][r, :]), "st", 16)
                if c >= 1:
                    env.w(eng, "ve", ve_u[c - 1])
                for nm, buf in (("d_ktanh", kb), ("d_v", vb)):
                    env.inc(eng.dma_start(out=buf[:], in_=io[nm][r, :]), "st", 16)
                env.inc(eng.dma_start(out=rb[c % 2][:], in_=io["d_r"][r, :]), "st", 16)
                if c >= 2:
                    env.w(eng, "po", po_gn[c - 2])
                env.inc(eng.dma_start(out=gb[c % 2][:], in_=io["d_g"][r, :]), "st", 16)
                assert env.cnt["st"] == st_ld[c]

        @env.block.vector
        def _(eng):
            env.fence(eng, snap)
            for c in range(C8):
                env.w(eng, "st", st_ld[c])
                eng.tensor_mul(ab[c % 2][:], ab[c % 2][:], kb[:])
                assert env.inc(eng.tensor_mul(ab[c % 2][:], ab[c % 2][:], vb[:]), "ve", 2) == ve_u[c]
                if c >= 2:
                    env.w(eng, "mm", mm_st[c - 2])
                    env.w(eng, "po", po_gn[c - 2])
                eng.tensor_tensor_scan(mx[c % 2][:], db[c % 2][:], ab[c % 2][:], 0.0, ALU.mult, ALU.add)
                assert env.inc(eng.tensor_mul(mx[c % 2][:], rb[c % 2][:], mx[c % 2][:]), "ve", 2) == ve_mx[c]

        @env.block.scalar
        def _(eng):
            env.fence(eng, snap)
            for s in range(10):
                if 2 <= s <= C8 + 1:
                    c = s - 2
                    env.w(eng, "mm", mm_bc[c])
                    if c >= 1:
                        env.w(eng, "po", po_gn[c - 1])
                    for t in range(NT):
                        eng.activation(mbc[:, t * TT:(t + 1) * TT], ps8[t][:], AF.Identity)
                        env.inc(eng.activation(rbc[:, t * TT:(t + 1) * TT], ps8[4 + t][:], AF.Identity),
                                "se", 2)
                    assert env.cnt["se"] == se_bc[c]
                if s < C8:
                    env.w(eng, "ve", ve_mx[s])
                    if s >= 1:
                        env.w(eng, "mm", mm_st[s - 1])
                    assert env.inc(eng.activation(sqb[:], mx[s % 2][:], AF.Square), "se") == se_sq[s]
                if 1 <= s <= C8:
                    c = s - 1
                    env.w(eng, "mm", mm_st[c])
                    for t in range(NT):
                        eng.activation(rows[:, t * TT:(t + 1) * TT], ps8[t][0:2, :], AF.Identity,
                                       scale=1.0 / 64.0)
                        env.inc(eng.activation(rows[:, T + t * TT: T + (t + 1) * TT], ps8[4 + t][0:2, :],
                                               AF.Identity, scale=1.0 / 64.0), "se", 2)
                    assert env.cnt["se"] == se_ev[c]
                    assert env.inc(eng.activation(rows[:, 2 * T:3 * T], rows[:, 0:T], AF.Square),
                                   "se") == se_msq[c]
                    env.w(eng, "po", po_var[c])
                    eng.activation(rows[:, 2 * T:3 * T], rows[:, T:2 * T], AF.Ln,
                                   bias=g["vt"][0:2, 0, V_EPSGN:V_EPSGN + 1])
                    assert env.inc(eng.activation(rows[:, 2 * T:3 * T], rows[:, 2 * T:3 * T], AF.Exp,
                                                  scale=-0.5), "se", 2) == se_rstd[c]

        @env.block.tensor
        def _(eng):
            env.fence(eng, snap)
            for s in range(10):
                if 1 <= s <= C8:
                    c = s - 1
                    env.w(eng, "se", se_rstd[c])
                    for t in range(NT):
                        eng.matmul(ps8[t][:], g["gsel"][:], rows[:, t * TT:(t + 1) * TT],
                                   start=True, stop=True)
                        env.inc(eng.matmul(ps8[4 + t][:], g["gsel"][:],
                                           rows[:, 2 * T + t * TT: 2 * T + (t + 1) * TT],
                                           start=True, stop=True), "mm", 2)
                    assert env.cnt["mm"] == mm_bc[c]
                if s < C8:
                    env.w(eng, "ve", ve_mx[s])
                    env.w(eng, "se", se_sq[s])
                    if s >= 1:
                        env.w(eng, "se", se_bc[s - 1])
                    for t in range(NT):
                        eng.matmul(ps8[t][0:2, :], g["gmask"][:], mx[s % 2][:, t * TT:(t + 1) * TT],
                                   start=True, stop=True)
                        env.inc(eng.matmul(ps8[4 + t][0:2, :], g["gmask"][:], sqb[:, t * TT:(t + 1) * TT],
                                           start=True, stop=True), "mm", 2)
                    assert env.cnt["mm"] == mm_st[s]

        @env.block.gpsimd
        def _(eng):
            env.fence(eng, snap)
            if wo_sb is not None:
                for c in range(C8):
                    env.inc(eng.dma_start(out=wo_sb[:, c * D:(c + 1) * D],
                                          in_=io["wout"][c * 128:(c + 1) * 128, :]), "gd", 16)
                env.pre_wout = env.cnt["gd"]
            for s in range(10):
                if 2 <= s <= C8 + 1:
                    c = s - 2
                    env.w(eng, "se", se_bc[c])
                    env.w(eng, "st", st_ld[c])
                    env.w(eng, "mm", mm_st[c])
                    n = env.inc(eng.tensor_sub(mx[c % 2][:], mx[c % 2][:], mbc[:]), "po")
                    eng.wait_ge(env.s("po"), n)
                    n = env.inc(eng.tensor_mul(mx[c % 2][:], mx[c % 2][:], rbc[:]), "po")
                    eng.wait_ge(env.s("po"), n)
                    n = env.inc(eng.tensor_scalar(mx[c % 2][:], mx[c % 2][:], g["vt"][:, c, V_GNW:V_GNW + 1],
                                                  g["vt"][:, c, V_GNB:V_GNB + 1], ALU.mult, ALU.add), "po")
                    eng.wait_ge(env.s("po"), n)
                    n = env.inc(eng.tensor_mul(mout[:, c * T:(c + 1) * T], mx[c % 2][:], gb[c % 2][:]), "po")
                    assert n == po_gn[c]
                    eng.wait_ge(env.s("po"), n)
                if 1 <= s <= C8:
                    c = s - 1
                    env.w(eng, "se", se_msq[c])
                    n = env.inc(eng.tensor_sub(rows[:, T:2 * T], rows[:, T:2 * T], rows[:, 2 * T:3 * T]),
                                "po")
                    assert n == po_var[c]
                    eng.wait_ge(env.s("po"), n)


def emit_l0_wout(env, io, g, mout, x1acc, wo, wo_gd, kact_sb=None, wfv_sb=None):
    snap = env.snap()
    with contextlib.ExitStack() as pc:
        E = pc.enter_context
        pmm = [E(env.nc.psum_tensor(uniq(f"wo_p{i}"), [128, TT], F32)) for i in range(2)]
        gd0, mm0, se0 = (snap.get(k, 0) for k in ("gd", "mm", "se"))
        wd = wo_gd
        mm_g = {}
        se_e = {}
        mm = mm0
        se = se0
        for t in range(NT):
            for o in range(C8):
                mm += 1
                mm_g[(t, o)] = mm
                se += 1
                se_e[(t, o)] = se

        @env.block.gpsimd
        def _(eng):
            env.fence(eng, snap)
            if wfv_sb is not None:
                for c in range(C8):
                    env.inc(eng.dma_start(out=wfv_sb[:, c * D:(c + 1) * D],
                                          in_=io["wfv"][c * 128:(c + 1) * 128, :]), "gd", 16)
            if kact_sb is not None:
                for c in range(C8):
                    env.inc(eng.dma_start(out=kact_sb[:, c * T:(c + 1) * T],
                                          in_=io["d_kact"][c * 128:(c + 1) * 128, :]), "gd", 16)
            env.pre_ffnv = env.cnt["gd"]

        @env.block.tensor
        def _(eng):
            env.fence(eng, snap)
            env.w(eng, "gd", wd)
            for t in range(NT):
                for o in range(C8):
                    gidx = t * C8 + o
                    if gidx >= 2:
                        pv = gidx - 2
                        env.w(eng, "se", se_e[(pv // C8, pv % C8)])
                    for c in range(C8):
                        ins = eng.matmul(pmm[gidx % 2][:],
                                         wo[:, c * D + o * 128: c * D + (o + 1) * 128],
                                         mout[:, c * T + t * TT: c * T + (t + 1) * TT],
                                         start=(c == 0), stop=(c == C8 - 1))
                    assert env.inc(ins, "mm") == mm_g[(t, o)]

        @env.block.scalar
        def _(eng):
            env.fence(eng, snap)
            for t in range(NT):
                for o in range(C8):
                    gidx = t * C8 + o
                    env.w(eng, "mm", mm_g[(t, o)])
                    ins = eng.activation(x1acc[:, o * T + t * TT: o * T + (t + 1) * TT],
                                         pmm[gidx % 2][:], AF.Identity,
                                         bias=g["vt"][:, o, V_BOUT:V_BOUT + 1])
                    assert env.inc(ins, "se") == se_e[(t, o)]


def emit_l0_ffnv(env, io, g, x1acc, x1_out, wf, ka, wd):
    snap = env.snap()
    with contextlib.ExitStack() as pc:
        E = pc.enter_context
        ostg = [E(env.nc.sbuf_tensor(uniq(f"fv_o{i}"), [128, TT], F32)) for i in range(2)]
        rfb = [E(env.nc.sbuf_tensor(uniq(f"fv_r{i}"), [128, TT], F32)) for i in range(2)]
        xfb = [E(env.nc.sbuf_tensor(uniq(f"fv_x{i}"), [128, T], F32)) for i in range(2)]
        pmm = [E(env.nc.psum_tensor(uniq(f"fv_p{i}"), [128, TT], F32)) for i in range(2)]
        gd0, mm0, se0, ve0, ax0, st0 = (snap.get(k, 0) for k in ("gd", "mm", "se", "ve", "ax", "st"))
        mm_g, se_e, ve_e, ax_l = {}, {}, {}, {}
        mm, se, ve, ax = mm0, se0, ve0, ax0
        for o in range(C8):
            for t in range(NT):
                mm += 1
                mm_g[(o, t)] = mm
                se += 1
                se_e[(o, t)] = se
                ax += 16
                ax_l[(o, t)] = ax
                ve += 2
                ve_e[(o, t)] = ve
        ax_x = {}
        ve_f = {}
        st_f = {}
        st = st0
        for c in range(C8):
            ax += 16
            ax_x[c] = ax
            ve += 1
            ve_f[c] = ve
            st += 16
            st_f[c] = st

        @env.block.tensor
        def _(eng):
            env.fence(eng, snap)
            env.w(eng, "gd", wd)
            for o in range(C8):
                for t in range(NT):
                    gidx = o * NT + t
                    if gidx >= 2:
                        pv = gidx - 2
                        env.w(eng, "se", se_e[(pv // NT, pv % NT)])
                    for c in range(C8):
                        ins = eng.matmul(pmm[gidx % 2][:],
                                         wf[:, c * D + o * 128: c * D + (o + 1) * 128],
                                         ka[:, c * T + t * TT: c * T + (t + 1) * TT],
                                         start=(c == 0), stop=(c == C8 - 1))
                    assert env.inc(ins, "mm") == mm_g[(o, t)]

        @env.block.scalar
        def _(eng):
            env.fence(eng, snap)
            for o in range(C8):
                for t in range(NT):
                    gidx = o * NT + t
                    if gidx >= 2:
                        pv = gidx - 2
                        env.w(eng, "ve", ve_e[(pv // NT, pv % NT)])
                    env.w(eng, "mm", mm_g[(o, t)])
                    ins = eng.activation(ostg[gidx % 2][:], pmm[gidx % 2][:], AF.Identity)
                    assert env.inc(ins, "se") == se_e[(o, t)]
                    ins = eng.dma_start(out=rfb[gidx % 2][:],
                                        in_=io["d_rf"][o * 128:(o + 1) * 128, t * TT:(t + 1) * TT])
                    ins.then_inc(env.s("ax"), 16)
                    env.cnt["ax"] += 16
            for c in range(C8):
                env.w(eng, "ve", ve_e[(C8 - 1, NT - 1)] if c == 0 else ve_f[c - 1])
                if c >= 2:
                    env.w(eng, "st", st_f[c - 2])
                ins = eng.dma_start(out=xfb[c % 2][:], in_=io["d_x1"][c * 128:(c + 1) * 128, :])
                ins.then_inc(env.s("ax"), 16)
                env.cnt["ax"] += 16

        @env.block.vector
        def _(eng):
            env.fence(eng, snap)
            for o in range(C8):
                for t in range(NT):
                    gidx = o * NT + t
                    env.w(eng, "se", se_e[(o, t)])
                    env.w(eng, "ax", ax_l[(o, t)])
                    eng.tensor_mul(ostg[gidx % 2][:], ostg[gidx % 2][:], rfb[gidx % 2][:])
                    dst = x1acc[:, o * T + t * TT: o * T + (t + 1) * TT]
                    env.inc(eng.tensor_add(dst, dst, ostg[gidx % 2][:]), "ve", 2)
            for c in range(C8):
                env.w(eng, "ax", ax_x[c])
                dst = x1acc[:, c * T:(c + 1) * T]
                assert env.inc(eng.tensor_add(dst, dst, xfb[c % 2][:]), "ve") == ve_f[c]

        @env.block.sync
        def _(eng):
            env.fence(eng, snap)
            for c in range(C8):
                env.w(eng, "ve", ve_f[c])
                ins = eng.dma_start(out=x1_out[c * 128:(c + 1) * 128, :], in_=x1acc[:, c * T:(c + 1) * T])
                assert env.inc(ins, "st", 16) == st_f[c]


def emit_l1_ln(env, io, g, xn16, xsb=None):
    snap = env.snap()
    with contextlib.ExitStack() as pc:
        E = pc.enter_context
        if xsb is None:
            x1sb = E(env.nc.sbuf_tensor(uniq("l1_x1"), [128, C8 * T], F32))
            ax0 = snap.get("ax", 0)

            @env.block.scalar
            def _(eng):
                env.fence(eng, snap)
                for c in range(C8):
                    ins = eng.dma_start(out=x1sb[:, c * T:(c + 1) * T],
                                        in_=io["d_x1"][c * 128:(c + 1) * 128, :])
                    ins.then_inc(env.s("ax"), 16)
                    env.cnt["ax"] += 16
            ld_done = env.cnt["ax"]

            @env.block.vector
            def _(eng):
                env.w(eng, "ax", ld_done)

            @env.block.tensor
            def _(eng):
                env.w(eng, "ax", ld_done)

            @env.block.scalar
            def _(eng):
                env.w(eng, "ax", ld_done)
        else:
            x1sb = xsb

        ve_done = emit_ln_fm(env, g, x1sb, V_EPSLN, rms=False, nw=False)

        @env.block.vector
        def _(eng):
            for c in range(C8):
                env.inc(eng.tensor_copy(xn16[:, c * T:(c + 1) * T], x1sb[:, c * T:(c + 1) * T]), "ve")
        ve16 = env.cnt["ve"]

        @env.block.sync
        def _(eng):
            env.fence(eng, snap)
            for c in range(C8):
                env.w(eng, "ve", ve_done[c])
                env.inc(eng.dma_start(out=io["d_z"][c * 128:(c + 1) * 128, :],
                                      in_=x1sb[:, c * T:(c + 1) * T]), "st", 16)
    return ve16


def emit_l1_attn(env, io, g, xn16):
    snap = env.snap()
    with contextlib.ExitStack() as pc:
        E = pc.enter_context
        qkv = E(env.nc.sbuf_tensor(uniq("at_qkv"), [128, C8, 3 * 64], BF16))
        wos = E(env.nc.sbuf_tensor(uniq("at_wo"), [64, D], BF16))
        q16 = E(env.nc.sbuf_tensor(uniq("at_q16"), [64, T], BF16))
        kf = E(env.nc.sbuf_tensor(uniq("at_kf"), [64, T], F32))
        vf = E(env.nc.sbuf_tensor(uniq("at_vf"), [64, T], F32))
        ktm = E(env.nc.sbuf_tensor(uniq("at_ktm"), [128, NTC * 64], BF16))
        vtm = E(env.nc.sbuf_tensor(uniq("at_vtm"), [128, NTC * 64], BF16))
        ef = E(env.nc.sbuf_tensor(uniq("at_ef"), [64, 64], F32))
        e16 = E(env.nc.sbuf_tensor(uniq("at_e16"), [64, 64], BF16))
        ctx16 = E(env.nc.sbuf_tensor(uniq("at_ctx"), [64, T], BF16))
        ostg = [E(env.nc.sbuf_tensor(uniq(f"at_o{i}"), [128, TT], F32)) for i in range(2)]
        xfb = [E(env.nc.sbuf_tensor(uniq(f"at_x{i}"), [128, TT], F32)) for i in range(2)]
        pq = [E(env.nc.psum_tensor(uniq(f"at_pq{i}"), [64, TT], F32)) for i in range(2)]
        ptr = [E(env.nc.psum_tensor(uniq(f"at_pt{i}"), [128, 64], F32)) for i in range(2)]
        pe = E(env.nc.psum_tensor(uniq("at_pe"), [64, 64], F32))
        po = [E(env.nc.psum_tensor(uniq(f"at_po{i}"), [128, TT], F32)) for i in range(2)]

        gd0, mm0, ve0, se0, ax0, st0 = (snap.get(k, 0) for k in ("gd", "mm", "ve", "se", "ax", "st"))
        wd = gd0 + 16 * 25
        mm_qkv = [mm0 + i + 1 for i in range(12)]
        se_qkv = [se0 + i + 1 for i in range(12)]
        mm_tp = [mm0 + 12 + i + 1 for i in range(32)]
        ve_tp = [ve0 + i + 1 for i in range(32)]
        mm_en = mm0 + 12 + 32 + 1
        ve_ef = ve0 + 32 + 1
        ve_e16 = ve0 + 32 + 2
        mm_cx = [mm_en + i + 1 for i in range(4)]
        se_cx = [se0 + 12 + i + 1 for i in range(4)]
        mm_ou = [mm_en + 4 + i + 1 for i in range(32)]
        se_ou = [se0 + 16 + i + 1 for i in range(32)]
        ax_ou = [ax0 + 16 * (i + 1) for i in range(32)]
        ve_ou = [ve_e16 + i + 1 for i in range(32)]
        st_ou = [st0 + 16 * (i + 1) for i in range(32)]

        @env.block.gpsimd
        def _(eng):
            env.fence(eng, snap)
            for j in range(3):
                for c in range(C8):
                    env.inc(eng.dma_start(out=qkv[:, c, j * 64:(j + 1) * 64],
                                          in_=io["wqkv"][j, c * 128:(c + 1) * 128, :]), "gd", 16)
            env.inc(eng.dma_start(out=wos[:], in_=io["wo"][:, :]), "gd", 16)

        @env.block.tensor
        def _(eng):
            env.fence(eng, snap)
            env.w(eng, "gd", wd)
            for j in range(3):
                for t in range(NT):
                    gi = j * NT + t
                    if gi >= 2:
                        env.w(eng, "se", se_qkv[gi - 2])
                    for c in range(C8):
                        ins = eng.matmul(pq[gi % 2][:], qkv[:, c, j * 64:(j + 1) * 64],
                                         xn16[:, c * T + t * TT: c * T + (t + 1) * TT],
                                         start=(c == 0), stop=(c == C8 - 1))
                    env.inc(ins, "mm")
            for j in range(2):
                src = kf if j == 0 else vf
                env.w(eng, "se", se_qkv[(j + 1) * NT + NT - 1])
                for tc in range(NTC):
                    gi = j * NTC + tc
                    if gi >= 2:
                        env.w(eng, "ve", ve_tp[gi - 2])
                    ins = eng.transpose(ptr[gi % 2][:], src[0:64, tc * TC:(tc + 1) * TC], g["idn"][0:64, 0:64])
                    env.inc(ins, "mm")
            env.w(eng, "ve", ve_tp[31])
            for tc in range(NTC):
                ins = eng.matmul(pe[:], ktm[:, tc * 64:(tc + 1) * 64], vtm[:, tc * 64:(tc + 1) * 64],
                                 start=(tc == 0), stop=(tc == NTC - 1))
            env.inc(ins, "mm")
            env.w(eng, "ve", ve_e16)
            for t in range(NT):
                if t >= 2:
                    env.w(eng, "se", se_cx[t - 2])
                ins = eng.matmul(pq[t % 2][:], e16[:], q16[:, t * TT:(t + 1) * TT], start=True, stop=True)
                env.inc(ins, "mm")
            for o in range(C8):
                for t in range(NT):
                    gi = o * NT + t
                    env.w(eng, "se", se_cx[t])
                    if gi >= 2:
                        env.w(eng, "se", se_ou[gi - 2])
                    ins = eng.matmul(po[gi % 2][:], wos[:, o * 128:(o + 1) * 128],
                                     ctx16[:, t * TT:(t + 1) * TT], start=True, stop=True)
                    env.inc(ins, "mm")

        @env.block.scalar
        def _(eng):
            env.fence(eng, snap)
            for j in range(3):
                for t in range(NT):
                    gi = j * NT + t
                    env.w(eng, "mm", mm_qkv[gi])
                    if j == 0:
                        dst = q16[:, t * TT:(t + 1) * TT]
                    elif j == 1:
                        dst = kf[0:64, t * TT:(t + 1) * TT]
                    else:
                        dst = vf[0:64, t * TT:(t + 1) * TT]
                    env.inc(eng.activation(dst, pq[gi % 2][0:64, :], AF.Identity), "se")
            for t in range(NT):
                env.w(eng, "mm", mm_cx[t])
                env.inc(eng.activation(ctx16[:, t * TT:(t + 1) * TT], pq[t % 2][0:64, :],
                                       AF.Identity, scale=CTX_SCALE), "se")
            for o in range(C8):
                for t in range(NT):
                    gi = o * NT + t
                    env.w(eng, "mm", mm_ou[gi])
                    if gi >= 2:
                        env.w(eng, "ve", ve_ou[gi - 2])
                        env.w(eng, "st", st_ou[gi - 2])
                    env.inc(eng.activation(ostg[gi % 2][:], po[gi % 2][:], AF.Identity), "se")
                    ins = eng.dma_start(out=xfb[gi % 2][:],
                                        in_=io["d_x1"][o * 128:(o + 1) * 128, t * TT:(t + 1) * TT])
                    ins.then_inc(env.s("ax"), 16)
                    env.cnt["ax"] += 16

        @env.block.vector
        def _(eng):
            env.fence(eng, snap)
            for j in range(2):
                dst = ktm if j == 0 else vtm
                for tc in range(NTC):
                    gi = j * NTC + tc
                    env.w(eng, "mm", mm_tp[gi])
                    env.inc(eng.tensor_copy(dst[:, tc * 64:(tc + 1) * 64], ptr[gi % 2][:]), "ve")
            env.w(eng, "mm", mm_en)
            env.inc(eng.tensor_copy(ef[:], pe[:]), "ve")
            env.inc(eng.tensor_copy(e16[:], ef[:]), "ve")
            for o in range(C8):
                for t in range(NT):
                    gi = o * NT + t
                    env.w(eng, "se", se_ou[gi])
                    env.w(eng, "ax", ax_ou[gi])
                    if gi >= 2:
                        env.w(eng, "st", st_ou[gi - 2])
                    env.inc(eng.tensor_add(ostg[gi % 2][:], ostg[gi % 2][:], xfb[gi % 2][:]), "ve")

        @env.block.sync
        def _(eng):
            env.fence(eng, snap)
            for o in range(C8):
                for t in range(NT):
                    gi = o * NT + t
                    env.w(eng, "ve", ve_ou[gi])
                    ins = eng.dma_start(out=io["d_acc"][o * 128:(o + 1) * 128, t * TT:(t + 1) * TT],
                                        in_=ostg[gi % 2][:])
                    assert env.inc(ins, "st", 16) == st_ou[gi]


def emit_l1_rms(env, io, g, lgt, zn16r, add_macc=False):
    snap = env.snap()
    with contextlib.ExitStack() as pc:
        E = pc.enter_context
        zsb = E(env.nc.sbuf_tensor(uniq("rm_z"), [128, C8 * T], F32))
        mbb = E(env.nc.sbuf_tensor(uniq("rm_mb"), [128, T], F32))
        sqb = E(env.nc.sbuf_tensor(uniq("rm_q"), [128, T], F32))

        rows = E(env.nc.sbuf_tensor(uniq("rm_rows"), [1, T], F32))
        rbc = E(env.nc.sbuf_tensor(uniq("rm_rbc"), [128, T], F32))
        wrt = E(env.nc.sbuf_tensor(uniq("rm_wrt"), [128, C8, 12], F32))
        lgf = E(env.nc.sbuf_tensor(uniq("rm_lgf"), [12, T], F32))
        psq = [E(env.nc.psum_tensor(uniq(f"rm_pq{i}"), [128, TT], F32)) for i in range(NT)]
        psr = [E(env.nc.psum_tensor(uniq(f"rm_pr{i}"), [128, TT], F32)) for i in range(NT)]

        gd0, mm0, ve0, se0, ax0, st0 = (snap.get(k, 0) for k in ("gd", "mm", "ve", "se", "ax", "st"))
        wd = gd0 + 16
        nld = 2 if add_macc else 1
        ax_z = [ax0 + 16 * nld * (c + 1) for c in range(C8)]
        nadd = 1 if add_macc else 0
        ve_add = [ve0 + nadd * (c + 1) for c in range(C8)]
        st_z = [st0 + 16 * (c + 1) for c in range(C8)] if add_macc else [0] * C8
        se_sq = [se0 + c + 1 for c in range(C8)]
        se_ev = se0 + 12
        se_rstd = se0 + 14
        se_rbc = se0 + 18
        se_lg = se0 + 22
        mm_st = [mm0 + 4 * (c + 1) for c in range(C8)]
        mm_bc = mm_st[-1] + 4
        mm_rt = [mm_bc + 4 * (c + 1) for c in range(C8)]
        mm_tp = [mm_rt[-1] + tc + 1 for tc in range(NTC)]
        vebase = ve0 + nadd * C8
        ve_mean = vebase + 1
        ve_zn = [vebase + 1 + 3 * c + 2 for c in range(C8)]
        ve_cast = [vebase + 1 + 3 * c + 3 for c in range(C8)]
        ve_tp = [ve_cast[-1] + tc + 1 for tc in range(NTC)]

        @env.block.gpsimd
        def _(eng):
            env.fence(eng, snap)
            env.inc(eng.dma_start(out=wrt[:], in_=io["wrt"].rearrange("(c p) v -> p c v", p=128)), "gd", 16)

        @env.block.scalar
        def _(eng):
            env.fence(eng, snap)
            for c in range(C8):
                ins = eng.dma_start(out=zsb[:, c * T:(c + 1) * T], in_=io["d_z"][c * 128:(c + 1) * 128, :])
                ins.then_inc(env.s("ax"), 16)
                env.cnt["ax"] += 16
                if add_macc:
                    if c >= 1:
                        env.w(eng, "ve", ve_add[c - 1])
                    ins = eng.dma_start(out=mbb[:], in_=io["d_macc"][:, c, :])
                    ins.then_inc(env.s("ax"), 16)
                    env.cnt["ax"] += 16
            for c in range(C8):
                if add_macc:
                    env.w(eng, "ve", ve_add[c])
                else:
                    env.w(eng, "ax", ax_z[c])
                if c >= 1:
                    env.w(eng, "mm", mm_st[c - 1])
                assert env.inc(eng.activation(sqb[:], zsb[:, c * T:(c + 1) * T], AF.Square), "se") == se_sq[c]
            env.w(eng, "mm", mm_st[-1])
            for t in range(NT):
                env.inc(eng.activation(rows[0:1, t * TT:(t + 1) * TT], psq[t][0:1, :], AF.Identity), "se")
            assert env.cnt["se"] == se_ev
            env.w(eng, "ve", ve_mean)
            eng.activation(rows[0:1, 0:T], rows[0:1, 0:T], AF.Ln,
                           bias=g["vt"][0:1, 0, V_EPSRMS:V_EPSRMS + 1])
            ins = eng.activation(rows[0:1, 0:T], rows[0:1, 0:T], AF.Exp, scale=-0.5)
            assert env.inc(ins, "se", 2) == se_rstd
            env.w(eng, "mm", mm_bc)
            for t in range(NT):
                env.inc(eng.activation(rbc[:, t * TT:(t + 1) * TT], psq[t][:], AF.Identity), "se")
            assert env.cnt["se"] == se_rbc
            env.w(eng, "mm", mm_rt[-1])
            for t in range(NT):
                env.inc(eng.activation(lgf[0:12, t * TT:(t + 1) * TT], psr[t][0:12, :], AF.Identity), "se")
            assert env.cnt["se"] == se_lg

        @env.block.vector
        def _(eng):
            env.fence(eng, snap)
            if add_macc:
                for c in range(C8):
                    env.w(eng, "ax", ax_z[c])
                    zc0 = zsb[:, c * T:(c + 1) * T]
                    assert env.inc(eng.tensor_add(zc0, zc0, mbb[:]), "ve") == ve_add[c]
            env.w(eng, "se", se_ev)
            assert env.inc(eng.tensor_scalar_mul(rows[0:1, 0:T], rows[0:1, 0:T], 1.0 / D), "ve") == ve_mean
            for c in range(C8):
                env.w(eng, "se", se_rbc)
                if add_macc:
                    env.w(eng, "st", st_z[c])
                zc = zsb[:, c * T:(c + 1) * T]
                eng.tensor_mul(zc, zc, rbc[:])
                env.inc(eng.tensor_scalar(zc, zc, g["vt"][:, c, V_NW:V_NW + 1], None, ALU.mult), "ve", 2)
                assert env.cnt["ve"] == ve_zn[c]
                assert env.inc(eng.tensor_copy(zn16r[:, c * T:(c + 1) * T], zc), "ve") == ve_cast[c]
            for tc in range(NTC):
                env.w(eng, "mm", mm_tp[tc])
                env.inc(eng.tensor_copy(lgt[:, tc, :], psq[tc % 2][0:128, 0:12]), "ve")
                assert env.cnt["ve"] == ve_tp[tc]

        @env.block.tensor
        def _(eng):
            env.fence(eng, snap)
            env.w(eng, "gd", wd)
            for c in range(C8):
                env.w(eng, "se", se_sq[c])
                for t in range(NT):
                    env.inc(eng.matmul(psq[t][0:1, :], g["onesk"][:], sqb[:, t * TT:(t + 1) * TT],
                                       start=(c == 0), stop=(c == C8 - 1)), "mm")
                assert env.cnt["mm"] == mm_st[c]
            env.w(eng, "ve", ve_mean)
            env.w(eng, "se", se_rstd)
            for t in range(NT):
                env.inc(eng.matmul(psq[t][:], g["onebc"][:], rows[0:1, t * TT:(t + 1) * TT],
                                   start=True, stop=True), "mm")
            assert env.cnt["mm"] == mm_bc
            for c in range(C8):
                env.w(eng, "ve", ve_zn[c])
                for t in range(NT):
                    env.inc(eng.matmul(psr[t][0:12, :], wrt[:, c, :], zsb[:, c * T + t * TT: c * T + (t + 1) * TT],
                                       start=(c == 0), stop=(c == C8 - 1)), "mm")
                assert env.cnt["mm"] == mm_rt[c]
            env.w(eng, "se", se_lg)
            env.w(eng, "se", se_rbc)
            for tc in range(NTC):
                if tc >= 2:
                    env.w(eng, "ve", ve_tp[tc - 2])
                ins = eng.transpose(psq[tc % 2][0:128, 0:12], lgf[0:12, tc * TC:(tc + 1) * TC],
                                    g["idn"][0:12, 0:12])
                env.inc(ins, "mm")
            assert env.cnt["mm"] == mm_tp[-1]

        if add_macc:
            @env.block.sync
            def _(eng):
                env.fence(eng, snap)
                for c in range(C8):
                    env.w(eng, "ve", ve_add[c])
                    assert env.inc(eng.dma_start(out=io["d_z"][c * 128:(c + 1) * 128, :],
                                                 in_=zsb[:, c * T:(c + 1) * T]), "st", 16) == st_z[c]

    return


def emit_l1_router(env, io, g, lgt, cmb_fm):
    snap = env.snap()
    from concourse.bass import broadcast_tensor_aps

    def bc(small, big):
        a, _ = broadcast_tensor_aps(small, big)
        return a

    with contextlib.ExitStack() as pc:
        E = pc.enter_context
        rw = E(env.nc.sbuf_tensor(uniq("rt_rw"), [128, NTC, 40], F32))
        cmbt = E(env.nc.sbuf_tensor(uniq("rt_cmbt"), [128, NTC, 8], F32))
        ptp = [E(env.nc.psum_tensor(uniq(f"rt_p{i}"), [8, 128], F32)) for i in range(2)]

        mm0, ve0, se0 = (snap.get(k, 0) for k in ("mm", "ve", "se"))

        @env.block.vector
        def _(eng):
            env.fence(eng, snap)

            def barrier():
                eng.wait_ge(env.s("ve"), env.cnt["ve"])

            lg4 = lgt[:, :, 0:4]
            env.inc(eng.reduce_max(rw[:, :, 0:1], lg4, AX), "ve")
            barrier()
            env.inc(eng.tensor_tensor(rw[:, :, 1:5], lg4, bc(rw[:, :, 0:1], lg4), ALU.subtract), "ve")
            env.w(eng, "se", se0 + 1)
            env.inc(eng.tensor_reduce(rw[:, :, 5:6], rw[:, :, 1:5], AX, ALU.add), "ve")
            barrier()
            env.inc(eng.reciprocal(rw[:, :, 6:7], rw[:, :, 5:6]), "ve")
            barrier()
            p = rw[:, :, 8:12]
            env.inc(eng.tensor_tensor(p, rw[:, :, 1:5], bc(rw[:, :, 6:7], p), ALU.mult), "ve")
            env.inc(eng.reduce_max(rw[:, :, 12:13], p, AX), "ve")
            barrier()
            env.inc(eng.tensor_tensor(rw[:, :, 13:17], p, bc(rw[:, :, 12:13], p), ALU.is_ge), "ve")
            env.inc(eng.scalar_tensor_tensor(rw[:, :, 17:21], rw[:, :, 13:17], -2.0, p,
                                             ALU.mult, ALU.add), "ve")
            env.inc(eng.reduce_max(rw[:, :, 21:22], rw[:, :, 17:21], AX), "ve")
            barrier()
            env.inc(eng.tensor_tensor(rw[:, :, 22:26], p, bc(rw[:, :, 21:22], p), ALU.is_ge), "ve")
            env.inc(eng.tensor_add(rw[:, :, 26:27], rw[:, :, 12:13], rw[:, :, 21:22]), "ve")
            barrier()
            env.inc(eng.reciprocal(rw[:, :, 27:28], rw[:, :, 26:27]), "ve")
            barrier()
            G = rw[:, :, 28:32]
            env.inc(eng.tensor_tensor(G, p, bc(rw[:, :, 27:28], G), ALU.mult), "ve")
            env.inc(eng.tensor_mul(G, G, rw[:, :, 22:26]), "ve")
            env.inc(eng.tensor_tensor(rw[:, :, 32:36], lgt[:, :, 5:12:2], lgt[:, :, 4:12:2], ALU.is_gt), "ve")
            env.inc(eng.tensor_scalar(rw[:, :, 36:40], rw[:, :, 32:36], -1.0, 1.0, ALU.mult, ALU.add), "ve")
            env.inc(eng.tensor_mul(cmbt[:, :, 0:8:2], G, rw[:, :, 36:40]), "ve")
            env.inc(eng.tensor_mul(cmbt[:, :, 1:8:2], G, rw[:, :, 32:36]), "ve")
        ve_cmb = env.cnt["ve"]

        @env.block.scalar
        def _(eng):
            env.fence(eng, snap)
            env.w(eng, "ve", ve0 + 2)
            env.inc(eng.activation(rw[:, :, 1:5], rw[:, :, 1:5], AF.Exp), "se")

        @env.block.tensor
        def _(eng):
            env.fence(eng, snap)
            env.w(eng, "ve", ve_cmb)
            for tc in range(NTC):
                if tc >= 2:
                    env.w(eng, "ve", ve_cmb + tc - 1)
                env.inc(eng.transpose(ptp[tc % 2][:], cmbt[:, tc, :], g["idn"][:]), "mm")

        @env.block.vector
        def _(eng):
            for tc in range(NTC):
                env.w(eng, "mm", mm0 + tc + 1)
                env.inc(eng.tensor_copy(cmb_fm[0:8, tc * TC:(tc + 1) * TC], ptp[tc % 2][:]), "ve")
    return env.cnt["ve"]


def emit_l1_experts(env, io, g, cmb_fm, zn16r, loop_idx, w1b0=None, wd0=None):
    snap = env.snap()
    with contextlib.ExitStack() as pc:
        E = pc.enter_context
        w1b = [w1b0 if (i == 0 and w1b0 is not None) else
               E(env.nc.sbuf_tensor(uniq(f"ex_w1{i}"), [128, C8 * 2048], BF16)) for i in range(2)]
        w2b = [E(env.nc.sbuf_tensor(uniq(f"ex_w2{i}"), [128, NF * D], BF16)) for i in range(2)]

        ht = E(env.nc.sbuf_tensor(uniq("ex_ht"), [128, NF * TT], BF16))
        yst = E(env.nc.sbuf_tensor(uniq("ex_yst"), [128, C8, TT], F32))
        cbb = [E(env.nc.sbuf_tensor(uniq(f"ex_cb{i}"), [128, TT], F32)) for i in range(2)]
        pcb = [E(env.nc.psum_tensor(uniq(f"ex_pcb{i}"), [128, TT], F32)) for i in range(2)]
        ph = [E(env.nc.psum_tensor(uniq(f"ex_ph{i}"), [128, TT], F32)) for i in range(2)]
        py = [E(env.nc.psum_tensor(uniq(f"ex_py{i}"), [128, TT], F32)) for i in range(2)]

        gd0, mm0, ve0, se0, st0 = (snap.get(k, 0) for k in ("gd", "mm", "ve", "se", "st"))

        mm_cb, mm_h, mm_he, mm_y, mm_e = {}, {}, {}, {}, {}
        se_h, se_he = {}, {}
        ve_cb, ve_y, ve_ye = {}, {}, {}
        gd_w, gd_w2, gd_mc = {}, {}, {}
        st_zn = {}
        mm = mm0
        se = se0
        ve = ve0
        gd = gd0
        st = st0
        for e in range(NEXP):
            if e < 2:
                if e == 0 and wd0 is not None:
                    gd_w[0] = wd0
                else:
                    gd += 16 * 8
                    gd_w[e] = gd
                gd += 16 * 16
                gd_w2[e] = gd
        for e in range(NEXP):
            for t in range(NT):
                mm += 1
                mm_cb[(e, t)] = mm
                for f in range(NF):
                    mm += 1
                    mm_h[(e, t, f)] = mm
                    se += 1
                    se_h[(e, t, f)] = se
                mm_he[(e, t)] = mm
                se_he[(e, t)] = se
                ve += 1
                ve_cb[(e, t)] = ve
                for d in range(C8):
                    mm += 1
                    mm_y[(e, t, d)] = mm
                    ve += 1
                    ve_y[(e, t, d)] = ve
                ve_ye[(e, t)] = ve
                gd += 16
                gd_mc[(e, t)] = gd
            mm_e[e] = mm
            if e + 2 < NEXP:
                gd += 16 * 8
                gd_w[e + 2] = gd
                gd += 16 * 16
                gd_w2[e + 2] = gd

        @env.block.gpsimd
        def _(eng):
            env.fence(eng, snap)
            for e in range(2):
                if not (e == 0 and wd0 is not None):
                    for c in range(C8):
                        env.inc(eng.dma_start(out=w1b[e % 2][:, c * 2048:(c + 1) * 2048],
                                              in_=io["w1"][e, c * 128:(c + 1) * 128, :]), "gd", 16)
                    assert env.cnt["gd"] == gd_w[e]
                for f in range(NF):
                    env.inc(eng.dma_start(out=w2b[e % 2][:, f * D:(f + 1) * D],
                                          in_=io["w2"][e, f * 128:(f + 1) * 128, :]), "gd", 16)
                assert env.cnt["gd"] == gd_w2[e]
            for e in range(NEXP):
                for t in range(NT):
                    env.w(eng, "ve", ve_ye[(e, t)])
                    src = yst[:]
                    dst = io["d_macc"][:, :, t * TT:(t + 1) * TT]
                    if e == 0:
                        ins = eng.dma_start(out=dst, in_=src)
                    else:
                        ins = eng.dma_start(out=dst, in_=src, accum_op=ALU.add)
                    assert env.inc(ins, "gd", 16) == gd_mc[(e, t)]
                if e + 2 < NEXP:
                    env.w(eng, "mm", mm_e[e])
                    for c in range(C8):
                        env.inc(eng.dma_start(out=w1b[e % 2][:, c * 2048:(c + 1) * 2048],
                                              in_=io["w1"][e + 2, c * 128:(c + 1) * 128, :]), "gd", 16)
                    assert env.cnt["gd"] == gd_w[e + 2]
                    for f in range(NF):
                        env.inc(eng.dma_start(out=w2b[e % 2][:, f * D:(f + 1) * D],
                                              in_=io["w2"][e + 2, f * 128:(f + 1) * 128, :]), "gd", 16)
                    assert env.cnt["gd"] == gd_w2[e + 2]

        @env.block.tensor
        def _(eng):
            env.fence(eng, snap)
            for e in range(NEXP):
                env.w(eng, "gd", gd_w[e])
                for t in range(NT):
                    pu = (e, t - 2) if t >= 2 else ((e - 1, t + 2) if e >= 1 else None)
                    if pu is not None:
                        env.w(eng, "ve", ve_cb[pu])
                    env.inc(eng.matmul(pcb[t % 2][:], g["oneh"][:, e * 128:(e + 1) * 128],
                                       cmb_fm[0:8, t * TT:(t + 1) * TT], start=True, stop=True), "mm")
                    for f in range(NF):
                        gi = (e * NT + t) * NF + f
                        if gi >= 2:
                            pv = gi - 2
                            pe_, rr = divmod(pv, NT * NF)
                            pt_, pf = divmod(rr, NF)
                            env.w(eng, "se", se_h[(pe_, pt_, pf)])
                        for c in range(C8):
                            ins = eng.matmul(ph[gi % 2][:],
                                             w1b[e % 2][:, c * 2048 + f * 128: c * 2048 + (f + 1) * 128],
                                             zn16r[:, c * T + t * TT: c * T + (t + 1) * TT],
                                             start=(c == 0), stop=(c == C8 - 1))
                        assert env.inc(ins, "mm") == mm_h[(e, t, f)]
                    env.w(eng, "se", se_he[(e, t)])
                    if t == 0:
                        env.w(eng, "gd", gd_w2[e])
                    for d in range(C8):
                        gi = (e * NT + t) * C8 + d
                        if gi >= 2:
                            pv = gi - 2
                            pe_, rr = divmod(pv, NT * C8)
                            pt_, pd = divmod(rr, C8)
                            env.w(eng, "ve", ve_y[(pe_, pt_, pd)])
                        for f in range(NF):
                            ins = eng.matmul(py[gi % 2][:],
                                             w2b[e % 2][:, f * D + d * 128: f * D + (d + 1) * 128],
                                             ht[:, f * TT:(f + 1) * TT],
                                             start=(f == 0), stop=(f == NF - 1))
                        assert env.inc(ins, "mm") == mm_y[(e, t, d)]

        @env.block.scalar
        def _(eng):
            env.fence(eng, snap)
            for e in range(NEXP):
                for t in range(NT):
                    for f in range(NF):
                        env.w(eng, "mm", mm_h[(e, t, f)])
                        ins = eng.activation(ht[:, f * TT:(f + 1) * TT], ph[((e * NT + t) * NF + f) % 2][:],
                                             AF.Silu)
                        assert env.inc(ins, "se") == se_h[(e, t, f)]

        @env.block.vector
        def _(eng):
            env.fence(eng, snap)
            for e in range(NEXP):
                for t in range(NT):
                    env.w(eng, "mm", mm_cb[(e, t)])
                    pu = (e, t - 2) if t >= 2 else ((e - 1, t + 2) if e >= 1 else None)
                    if pu is not None:
                        env.w(eng, "ve", ve_ye[pu])
                    assert env.inc(eng.tensor_copy(cbb[t % 2][:], pcb[t % 2][:]), "ve") == ve_cb[(e, t)]
                    for d in range(C8):
                        env.w(eng, "mm", mm_y[(e, t, d)])
                        if d == 0:
                            pu2 = (e, t - 1) if t >= 1 else ((e - 1, NT - 1) if e >= 1 else None)
                            if pu2 is not None:
                                env.w(eng, "gd", gd_mc[pu2])
                        ins = eng.tensor_tensor(yst[:, d, :], py[((e * NT + t) * C8 + d) % 2][:],
                                                cbb[t % 2][:], ALU.mult)
                        assert env.inc(ins, "ve") == ve_y[(e, t, d)]
    return


def emit_l1_zupd(env, io, g, last, fs_sb):
    assert last
    snap = env.snap()
    with contextlib.ExitStack() as pc:
        E = pc.enter_context
        zb = [E(env.nc.sbuf_tensor(uniq(f"zu_z{i}"), [128, T], F32)) for i in range(2)]
        mb = [E(env.nc.sbuf_tensor(uniq(f"zu_m{i}"), [128, T], F32)) for i in range(2)]
        ab = [E(env.nc.sbuf_tensor(uniq(f"zu_a{i}"), [128, T], F32)) for i in range(2)]
        ax0, ve0, st0, gd0 = (snap.get(k, 0) for k in ("ax", "ve", "st", "gd"))
        ax_at = [ax0 + 16 * (c + 1) for c in range(C8)]
        st_at = [st0 + 16 * (c + 1) for c in range(C8)]
        gd_at = [gd0 + 16 * (c + 1) for c in range(C8)]
        ve_at = [ve0 + 2 * (c + 1) for c in range(C8)]

        @env.block.scalar
        def _(eng):
            env.fence(eng, snap)
            for c in range(C8):
                if c >= 2:
                    env.w(eng, "ve", ve_at[c - 2])
                ins = eng.dma_start(out=zb[c % 2][:], in_=io["d_z"][c * 128:(c + 1) * 128, :])
                ins.then_inc(env.s("ax"), 16)
                env.cnt["ax"] += 16

        @env.block.sync
        def _(eng):
            env.fence(eng, snap)
            for c in range(C8):
                if c >= 2:
                    env.w(eng, "ve", ve_at[c - 2])
                env.inc(eng.dma_start(out=mb[c % 2][:], in_=io["d_macc"][:, c, :]), "st", 16)

        @env.block.gpsimd
        def _(eng):
            env.fence(eng, snap)
            for c in range(C8):
                if c >= 2:
                    env.w(eng, "ve", ve_at[c - 2])
                env.inc(eng.dma_start(out=ab[c % 2][:], in_=io["d_acc"][c * 128:(c + 1) * 128, :]), "gd", 16)

        @env.block.vector
        def _(eng):
            env.fence(eng, snap)
            for c in range(C8):
                env.w(eng, "ax", ax_at[c])
                env.w(eng, "st", st_at[c])
                env.w(eng, "gd", gd_at[c])
                env.inc(eng.tensor_add(zb[c % 2][:], zb[c % 2][:], mb[c % 2][:]), "ve")
                env.inc(eng.tensor_add(fs_sb[:, c * T:(c + 1) * T], zb[c % 2][:], ab[c % 2][:]), "ve")


def emit_fin(env, io, g, fs_sb):
    snap = env.snap()
    with contextlib.ExitStack() as pc:
        E = pc.enter_context
        otm = [E(env.nc.sbuf_tensor(uniq(f"fn_o{i}"), [128, D], F32)) for i in range(2)]
        ptf = [E(env.nc.psum_tensor(uniq(f"fn_p{i}"), [128, 128], F32)) for i in range(4)]
        mm0, ve0, se0, st0 = (snap.get(k, 0) for k in ("mm", "ve", "se", "st"))
        mm_tp = {}
        ve_ev = {}
        se_ev = {}
        st_at = [st0 + 16 * (i + 1) for i in range(NTC)]
        mm = mm0
        ve = ve0
        se = se0
        for tc in range(NTC):
            for c in range(C8):
                mm += 1
                mm_tp[(tc, c)] = mm
                if c % 2 == 0:
                    ve += 1
                    ve_ev[(tc, c)] = ve
                else:
                    se += 1
                    se_ev[(tc, c)] = se

        @env.block.tensor
        def _(eng):
            env.fence(eng, snap)
            for tc in range(NTC):
                for c in range(C8):
                    gi = tc * C8 + c
                    if gi >= 4:
                        pv = gi - 4
                        ptc, pcb = divmod(pv, C8)
                        if pcb % 2 == 0:
                            env.w(eng, "ve", ve_ev[(ptc, pcb)])
                        else:
                            env.w(eng, "se", se_ev[(ptc, pcb)])
                    ins = eng.transpose(ptf[gi % 4][:], fs_sb[:, c * T + tc * TC: c * T + (tc + 1) * TC],
                                        g["idn"][:])
                    assert env.inc(ins, "mm") == mm_tp[(tc, c)]

        @env.block.vector
        def _(eng):
            env.fence(eng, snap)
            for tc in range(NTC):
                if tc >= 2:
                    env.w(eng, "st", st_at[tc - 2])
                for c in range(0, C8, 2):
                    env.w(eng, "mm", mm_tp[(tc, c)])
                    gi = tc * C8 + c
                    assert env.inc(eng.tensor_copy(otm[tc % 2][:, c * 128:(c + 1) * 128], ptf[gi % 4][:]),
                                   "ve") == ve_ev[(tc, c)]

        @env.block.scalar
        def _(eng):
            env.fence(eng, snap)
            for tc in range(NTC):
                if tc >= 2:
                    env.w(eng, "st", st_at[tc - 2])
                for c in range(1, C8, 2):
                    env.w(eng, "mm", mm_tp[(tc, c)])
                    gi = tc * C8 + c
                    assert env.inc(eng.activation(otm[tc % 2][:, c * 128:(c + 1) * 128], ptf[gi % 4][:],
                                                  AF.Identity), "se") == se_ev[(tc, c)]

        @env.block.sync
        def _(eng):
            env.fence(eng, snap)
            for tc in range(NTC):
                for c in range(0, C8, 2):
                    env.w(eng, "ve", ve_ev[(tc, c)])
                for c in range(1, C8, 2):
                    env.w(eng, "se", se_ev[(tc, c)])
                ins = eng.dma_start(out=io["out"][tc * TC:(tc + 1) * TC, :], in_=otm[tc % 2][:])
                assert env.inc(ins, "st", 16) == st_at[tc]
            env.w(eng, "st", st_at[-1])


def build(mode):
    nc = bass.Bass("TRN2", target_bir_lowering=False, debug=False)
    io = {}

    def ei(name, shape, dtype=F32):
        io[name] = nc.dram_tensor(name, shape, dtype, kind="ExternalInput").ap()

    def eo(name, shape, dtype=F32):
        io[name] = nc.dram_tensor(name, shape, dtype, kind="ExternalOutput").ap()

    def itn(name, shape, dtype=F32):
        io[name] = nc.dram_tensor(name, shape, dtype, kind="Internal").ap()

    ei("vecs", [D, NVEC])
    ei("idn", [128, 128])
    ei("gmask", [128, 2])
    ei("gsel", [2, 128])
    ei("oneh", [8, 8 * 128])
    ei("onesk", [128, 1])
    ei("onebc", [1, 128])

    has_l0 = mode in ("l0", "all")
    has_l1 = mode in ("l1", "all")

    if has_l0:
        ei("x", [T, D])
        ei("wp", [8, D, D])
        ei("wout", [D, D])
        ei("wfv", [D, D])
        mk = eo if DEBUG_L0 else itn
        for n in ("d_decay", "d_ktanh", "d_v", "d_a", "d_r", "d_rf"):
            mk(n, [D, T])
        mk("d_g", [D, T], BF16)
        mk("d_kact", [D, T], BF16)
        if DEBUG_L0:
            eo("d_xn", [D, T])
            eo("d_mout", [D, T], BF16)
    if has_l1:
        ei("wqkv", [3, D, 64])
        ei("wo", [64, D])
        ei("wrt", [D, 12])
        ei("w1", [NEXP, D, 2048])
        ei("w2", [NEXP, 2048, D])
        itn("d_z", [D, T])
        itn("d_macc", [128, C8, T])
        itn("d_acc", [D, T])
        eo("out", [T, D])
    if mode == "l0":
        eo("x1", [D, T])
        io["d_x1"] = io["x1"]
    elif mode == "l1":
        ei("x1", [D, T])
        io["d_x1"] = io["x1"]
    else:
        itn("d_x1", [D, T])

    with contextlib.ExitStack() as ctx:
        block = ctx.enter_context(nc.Block())
        env = Env(nc, ctx, block)
        for s in FENCE_SEMS:
            env.s(s)
        g = emit_setup(env, io)

        if has_l0:
            with contextlib.ExitStack() as l0ctx:
                xsb = l0ctx.enter_context(nc.sbuf_tensor(uniq("l0_x"), [128, C8 * T], F32))
                wb = [l0ctx.enter_context(nc.sbuf_tensor(uniq(f"l0_wb{i}"), [128, C8 * D], BF16))
                      for i in range(2)]
                emit_tp_in(env, io, g, xsb)
                wd_pre = []

                @env.block.gpsimd
                def _(eng):
                    for p in range(2):
                        for c in range(C8):
                            env.inc(eng.dma_start(out=wb[p][:, c * D:(c + 1) * D],
                                                  in_=io["wp"][p, c * 128:(c + 1) * 128, :]), "gd", 16)
                        wd_pre.append(env.cnt["gd"])
                emit_ln_fm(env, g, xsb, V_EPSLN, rms=False)
                if DEBUG_L0:
                    snap_dbg = env.snap()

                    @env.block.sync
                    def _(eng):
                        env.fence(eng, snap_dbg)
                        for c in range(C8):
                            env.inc(eng.dma_start(out=io["d_xn"][c * 128:(c + 1) * 128, :],
                                                  in_=xsb[:, c * T:(c + 1) * T]), "st", 16)
                        env.w(eng, "st", env.cnt["st"])
                emit_l0_proj(env, io, g, xsb, wb=wb, wd_pre=wd_pre)
            with contextlib.ExitStack() as l0ctx2:
                mout = l0ctx2.enter_context(nc.sbuf_tensor(uniq("l0_mout"), [128, C8 * T], BF16))
                wo_sb = l0ctx2.enter_context(nc.sbuf_tensor(uniq("l0_wo"), [128, C8 * D], BF16))
                emit_l0_scan(env, io, g, mout, wo_sb=wo_sb)
                if DEBUG_L0:
                    snap_dbg2 = env.snap()

                    @env.block.sync
                    def _(eng):
                        env.fence(eng, snap_dbg2)
                        for c in range(C8):
                            env.inc(eng.dma_start(out=io["d_mout"][c * 128:(c + 1) * 128, :],
                                                  in_=mout[:, c * T:(c + 1) * T]), "st", 16)
                        env.w(eng, "st", env.cnt["st"])
                x1acc = l0ctx2.enter_context(nc.sbuf_tensor(uniq("l0_x1a"), [128, C8 * T], F32))
                wfv_sb = l0ctx2.enter_context(nc.sbuf_tensor(uniq("l0_wfv"), [128, C8 * D], BF16))
                kact_sb = l0ctx2.enter_context(nc.sbuf_tensor(uniq("l0_ka"), [128, C8 * T], BF16))
                emit_l0_wout(env, io, g, mout, x1acc, wo_sb, env.pre_wout,
                             kact_sb=kact_sb, wfv_sb=wfv_sb)
                emit_l0_ffnv(env, io, g, x1acc, io["d_x1"], wfv_sb, kact_sb, env.pre_ffnv)
            if mode == "l0":
                snap = env.snap()

                @env.block.sync
                def _(eng):
                    env.fence(eng, snap)

        if has_l1:
            with contextlib.ExitStack() as l1ctx:
                xn16 = l1ctx.enter_context(nc.sbuf_tensor(uniq("l1_xn16"), [128, C8 * T], BF16))
                emit_l1_ln(env, io, g, xn16)
                emit_l1_attn(env, io, g, xn16)
            with contextlib.ExitStack() as l1ctx2:
                zn16r = l1ctx2.enter_context(nc.sbuf_tensor(uniq("l1_zn16"), [128, C8 * T], BF16))
                lgt = l1ctx2.enter_context(nc.sbuf_tensor(uniq("l1_lgt"), [128, NTC, 12], F32))
                cmb = l1ctx2.enter_context(nc.sbuf_tensor(uniq("l1_cmb"), [8, T], BF16))
                w1b0 = l1ctx2.enter_context(nc.sbuf_tensor(uniq("l1_w10"), [128, C8 * 2048], BF16))
                for l in range(2):
                    mm_before = env.cnt["mm"]
                    snap_pf = env.snap()

                    @env.block.gpsimd
                    def _(eng):
                        env.fence(eng, snap_pf)
                        for c in range(C8):
                            env.inc(eng.dma_start(out=w1b0[:, c * 2048:(c + 1) * 2048],
                                                  in_=io["w1"][0, c * 128:(c + 1) * 128, :]), "gd", 16)
                    wd0 = env.cnt["gd"]
                    emit_l1_rms(env, io, g, lgt, zn16r, add_macc=(l == 1))
                    emit_l1_router(env, io, g, lgt, cmb)
                    emit_l1_experts(env, io, g, cmb, zn16r, l, w1b0=w1b0, wd0=wd0)
                fs_sb = l1ctx2.enter_context(nc.sbuf_tensor(uniq("l1_fs"), [128, C8 * T], F32))
                emit_l1_zupd(env, io, g, last=True, fs_sb=fs_sb)
                emit_fin(env, io, g, fs_sb)
    return nc


def _host_prep(inputs):
    f = np.float32
    t = lambda a: np.ascontiguousarray(np.asarray(a, dtype=f).T)
    wp = np.stack([
        t(inputs["L0_attn_W_w"]), t(inputs["L0_attn_W_k"]), t(inputs["L0_attn_W_v"]),
        t(inputs["L0_attn_W_a"]), t(inputs["L0_attn_W_r"]), t(inputs["L0_attn_W_g"]),
        t(inputs["L0_ffn_W_r"]), t(inputs["L0_ffn_W_k"]),
    ])
    vecs = np.zeros((D, NVEC), f)
    vecs[:, V_XW] = np.asarray(inputs["L0_attn_x_w"], f).reshape(-1)
    vecs[:, V_XK] = np.asarray(inputs["L0_attn_x_k"], f).reshape(-1)
    vecs[:, V_XV] = np.asarray(inputs["L0_attn_x_v"], f).reshape(-1)
    vecs[:, V_XA] = np.asarray(inputs["L0_attn_x_a"], f).reshape(-1)
    vecs[:, V_XR] = np.asarray(inputs["L0_attn_x_r"], f).reshape(-1)
    vecs[:, V_XG] = np.asarray(inputs["L0_attn_x_g"], f).reshape(-1)
    vecs[:, V_MUR] = np.asarray(inputs["L0_ffn_mu_r"], f).reshape(-1)
    vecs[:, V_MUK] = np.asarray(inputs["L0_ffn_mu_k"], f).reshape(-1)
    vecs[:, V_BOUT] = np.asarray(inputs["L0_attn_b_out"], f).reshape(-1)
    vecs[:, V_GNW] = np.asarray(inputs["L0_attn_gn_w"], f).reshape(-1)
    vecs[:, V_GNB] = np.asarray(inputs["L0_attn_gn_b"], f).reshape(-1)
    vecs[:, V_NW] = np.asarray(inputs["L1_ffn_norm_w"], f).reshape(-1)
    vecs[:, V_EPSLN] = EPS_LN
    vecs[:, V_EPSGN] = EPS_GN
    vecs[:, V_EPSRMS] = EPS_RMS

    wqkv = np.stack([t(inputs["L1_attn_wq"]), t(inputs["L1_attn_wk"]), t(inputs["L1_attn_wv"])])
    wo = t(inputs["L1_attn_wo"])
    wrt = np.zeros((D, 12), f)
    wm = np.asarray(inputs["L1_ffn_w_macro"], f)
    wu = np.asarray(inputs["L1_ffn_w_micro"], f)
    for gidx in range(4):
        wrt[:, gidx] = wm[gidx]
        for j in range(2):
            wrt[:, 4 + 2 * gidx + j] = wu[gidx, j]
    w1 = np.ascontiguousarray(np.asarray(inputs["L1_ffn_w1"], f).transpose(0, 2, 1))
    w2 = np.ascontiguousarray(np.asarray(inputs["L1_ffn_w2"], f).transpose(0, 2, 1))

    consts = {
        "idn": np.eye(128, dtype=f),
        "gmask": np.zeros((128, 2), f),
        "gsel": np.zeros((2, 128), f),
        "oneh": np.zeros((8, 8 * 128), f),
        "onesk": np.ones((128, 1), f),
        "onebc": np.ones((1, 128), f),
    }
    consts["gmask"][:64, 0] = 1.0
    consts["gmask"][64:, 1] = 1.0
    consts["gsel"][0, :64] = 1.0
    consts["gsel"][1, 64:] = 1.0
    for e in range(8):
        consts["oneh"][e, e * 128:(e + 1) * 128] = 1.0

    shared = dict(consts)
    shared.update(vecs=vecs, wp=wp, wout=t(inputs["L0_attn_W_out"]), wfv=t(inputs["L0_ffn_W_v"]),
                  wqkv=wqkv, wo=wo, wrt=wrt, w1=w1, w2=w2)
    return shared


L0_KEYS = ("vecs", "idn", "gmask", "gsel", "oneh", "onesk", "onebc", "wp", "wout", "wfv")
L1_KEYS = ("vecs", "idn", "gmask", "gsel", "oneh", "onesk", "onebc",
           "wqkv", "wo", "wrt", "w1", "w2")

MODE = "fused"


def kernel(**inputs):
    shared = _host_prep(inputs)
    x = np.ascontiguousarray(np.asarray(inputs["L0_attn_x"], np.float32))
    ncores = 8
    if MODE == "fused":
        nc = build("all")
        maps = []
        for b in range(ncores):
            m = {k: shared[k] for k in set(L0_KEYS + L1_KEYS)}
            m["x"] = x[b]
            maps.append(m)
        res = run_bass_kernel_spmd(nc, maps, core_ids=list(range(ncores)))
        return np.stack([res.results[b]["out"] for b in range(ncores)])
    else:
        ncA = build("l0")
        mapsA = []
        for b in range(ncores):
            m = {k: shared[k] for k in L0_KEYS}
            m["x"] = x[b]
            mapsA.append(m)
        resA = run_bass_kernel_spmd(ncA, mapsA, core_ids=list(range(ncores)))
        ncB = build("l1")
        mapsB = []
        for b in range(ncores):
            m = {k: shared[k] for k in L1_KEYS}
            m["x1"] = np.ascontiguousarray(resA.results[b]["x1"])
            mapsB.append(m)
        resB = run_bass_kernel_spmd(ncB, mapsB, core_ids=list(range(ncores)))
        return np.stack([resB.results[b]["out"] for b in range(ncores)])


# version 15
# speedup vs baseline: 1.1418x; 1.1418x over previous
import contextlib
import numpy as np
import concourse.bass as bass
import concourse.mybir as mybir
from concourse.bass_utils import run_bass_kernel_spmd

F32 = mybir.dt.float32
BF16 = mybir.dt.bfloat16
AF = mybir.ActivationFunctionType
ALU = mybir.AluOpType
AX = mybir.AxisListType.X

T = 2048
D = 1024
C8 = 8
TT = 512
NT = 4
TC = 128
NTC = 16
NF = 16
NEXP = 8
EPS_LN = 1e-5
EPS_GN = 0.00064
EPS_RMS = 1e-6
CTX_SCALE = float(np.sqrt(3.0 / 64.0))

V_XW, V_XK, V_XV, V_XA, V_XR, V_XG, V_MUR, V_MUK, V_BOUT, V_GNW, V_GNB, V_NW, V_EPSLN, V_EPSGN, V_EPSRMS = range(15)
NVEC = 16
PROJ = [
    ("w", V_XW, False, AF.Sigmoid, -1.0, False, "d_decay"),
    ("k", V_XK, False, AF.Tanh, 1.0, False, "d_ktanh"),
    ("v", V_XV, False, AF.Identity, 1.0, False, "d_v"),
    ("a", V_XA, False, AF.Sigmoid, 1.0, False, "d_a"),
    ("r", V_XR, False, AF.Sigmoid, 1.0, False, "d_r"),
    ("g", V_XG, False, AF.Sigmoid, 1.0, True, "d_g"),
    ("fr", V_MUR, True, AF.Sigmoid, 1.0, False, "d_rf"),
    ("fk", V_MUK, True, AF.Relu, 1.0, False, "d_kact"),
]

FENCE_SEMS = ("mm", "ve", "se", "st", "ax", "gd", "po")
DEBUG_L0 = False


class Env:
    def __init__(self, nc, ctx, block):
        self.nc = nc
        self.ctx = ctx
        self.block = block
        self._sems = {}
        self.cnt = {}

    def s(self, name):
        if name not in self._sems:
            self._sems[name] = self.ctx.enter_context(self.nc.semaphore(name))
            self.cnt[name] = 0
        return self._sems[name]

    def inc(self, ins, name, k=1):
        ins.then_inc(self.s(name), k)
        self.cnt[name] += k
        return self.cnt[name]

    def w(self, eng, name, val):
        if val > 0:
            eng.wait_ge(self.s(name), val)

    def bar(self, eng, ins, name="ve"):
        c = self.inc(ins, name)
        eng.wait_ge(self.s(name), c)

    def snap(self):
        return dict(self.cnt)

    def fence(self, eng, snap):
        for n in FENCE_SEMS:
            v = snap.get(n, 0)
            if v > 0:
                eng.wait_ge(self.s(n), v)


_UNIQ = [0]


def uniq(name):
    _UNIQ[0] += 1
    return f"{name}_{_UNIQ[0]}"


def sb(env, name, shape, dtype):
    return env.ctx.enter_context(env.nc.sbuf_tensor(name, shape, dtype))


def ps(env, name, shape, dtype=F32):
    return env.ctx.enter_context(env.nc.psum_tensor(name, shape, dtype))


def emit_setup(env, io):
    g = {}
    g["vt"] = sb(env, "sb_vt", [128, C8, NVEC], F32)
    g["cm2"] = sb(env, "cm2", [128, C8, 8], F32)
    g["idn"] = sb(env, "sb_idn", [128, 128], F32)
    g["gmask"] = sb(env, "sb_gmask", [128, 2], F32)
    g["gsel"] = sb(env, "sb_gsel", [2, 128], F32)
    g["oneh"] = sb(env, "sb_oneh", [8, 8 * 128], BF16)
    g["onesk"] = sb(env, "sb_onesk", [128, 1], F32)
    g["onebc"] = sb(env, "sb_onebc", [1, 128], F32)

    @env.block.gpsimd
    def _(eng):
        env.inc(eng.dma_start(out=g["vt"][:], in_=io["vecs"].rearrange("(c p) v -> p c v", p=128)), "gd", 16)
        env.inc(eng.dma_start(out=g["idn"][:], in_=io["idn"][:, :]), "gd", 16)
        env.inc(eng.dma_start(out=g["gmask"][:], in_=io["gmask"][:, :]), "gd", 16)
        env.inc(eng.dma_start(out=g["gsel"][:], in_=io["gsel"][:, :]), "gd", 16)
        env.inc(eng.dma_start(out=g["oneh"][:], in_=io["oneh"][:, :]), "gd", 16)
        env.inc(eng.dma_start(out=g["onesk"][:], in_=io["onesk"][:, :]), "gd", 16)
        env.inc(eng.dma_start(out=g["onebc"][:], in_=io["onebc"][:, :]), "gd", 16)
    gd_done = env.cnt["gd"]

    @env.block.vector
    def _(eng):
        env.w(eng, "gd", gd_done)
        env.inc(eng.tensor_scalar(g["cm2"][:], g["vt"][:, :, 0:8], -1.0, 1.0, ALU.mult, ALU.add), "ve")
    return g


def emit_tp_in(env, io, g, x_fm):
    snap = env.snap()
    with contextlib.ExitStack() as pc:
        xtm = [pc.enter_context(env.nc.sbuf_tensor(uniq(f"ig_xtm{i}"), [128, D], F32)) for i in range(2)]
        xstg = [pc.enter_context(env.nc.sbuf_tensor(uniq(f"ig_xstg{i}"), [128, C8, TC], F32)) for i in range(2)]
        pt = [pc.enter_context(env.nc.psum_tensor(uniq(f"ig_pt{i}"), [128, TC], F32)) for i in range(4)]

        gd0, mm0, ve0, se0, st0 = (snap.get(k, 0) for k in ("gd", "mm", "ve", "se", "st"))
        ld_at = [gd0 + 16 * (i + 1) for i in range(NTC)]
        mm_tp = [mm0 + 8 * (i + 1) for i in range(NTC)]
        ve_ev = [ve0 + 8 * (i + 1) for i in range(NTC)]
        se_ev = [se0 + 8 * (i + 1) for i in range(NTC)]
        st_at = [st0 + 16 * (i + 1) for i in range(NTC)]

        @env.block.gpsimd
        def _(eng):
            env.fence(eng, snap)
            for tc in range(NTC):
                if tc >= 2:
                    env.w(eng, "mm", mm_tp[tc - 2])
                env.inc(eng.dma_start(out=xtm[tc % 2][:], in_=io["x"][tc * TC:(tc + 1) * TC, :]), "gd", 16)

        @env.block.tensor
        def _(eng):
            env.fence(eng, snap)
            for tc in range(NTC):
                env.w(eng, "gd", ld_at[tc])
                for cb in range(C8):
                    idx = tc * 8 + cb
                    if idx >= 4:
                        pv_tc, pv_cb = divmod(idx - 4, 8)
                        if pv_cb % 2 == 0:
                            env.w(eng, "ve", ve0 + pv_tc * 8 + (pv_cb // 2) * 2 + 1)
                        else:
                            env.w(eng, "se", se0 + pv_tc * 8 + (pv_cb // 2) * 2 + 1)
                    ins = eng.transpose(pt[cb % 4][:], xtm[tc % 2][:, cb * 128:(cb + 1) * 128], g["idn"][:])
                    env.inc(ins, "mm")

        @env.block.vector
        def _(eng):
            env.fence(eng, snap)
            for tc in range(NTC):
                if tc >= 2:
                    env.w(eng, "st", st_at[tc - 2])
                for cb in range(0, C8, 2):
                    env.w(eng, "mm", mm0 + tc * 8 + cb + 1)
                    eng.tensor_copy(x_fm[:, cb * T + tc * TC: cb * T + (tc + 1) * TC], pt[cb % 4][:])
                    env.inc(eng.tensor_copy(xstg[tc % 2][:, cb, :], pt[cb % 4][:]), "ve", 2)

        @env.block.scalar
        def _(eng):
            env.fence(eng, snap)
            for tc in range(NTC):
                if tc >= 2:
                    env.w(eng, "st", st_at[tc - 2])
                for cb in range(1, C8, 2):
                    env.w(eng, "mm", mm0 + tc * 8 + cb + 1)
                    eng.activation(x_fm[:, cb * T + tc * TC: cb * T + (tc + 1) * TC], pt[cb % 4][:], AF.Identity)
                    env.inc(eng.activation(xstg[tc % 2][:, cb, :], pt[cb % 4][:], AF.Identity), "se", 2)

        @env.block.sync
        def _(eng):
            env.fence(eng, snap)
            for tc in range(NTC):
                env.w(eng, "ve", ve_ev[tc])
                env.w(eng, "se", se_ev[tc])
                ins = eng.dma_start(
                    out=io["d_x1"].rearrange("(c p) n -> p c n", p=128)[:, :, tc * TC:(tc + 1) * TC],
                    in_=xstg[tc % 2][:])
                env.inc(ins, "st", 16)


def emit_ln_fm(env, g, xsb, eps_col, rms=False, nw=False):
    snap = env.snap()
    ve_done = []
    with contextlib.ExitStack() as pc:
        psA = [pc.enter_context(env.nc.psum_tensor(uniq(f"ln_a{i}"), [128, TT], F32)) for i in range(NT)]
        psB = [pc.enter_context(env.nc.psum_tensor(uniq(f"ln_b{i}"), [128, TT], F32)) for i in range(NT)]
        sqb = [pc.enter_context(env.nc.sbuf_tensor(uniq(f"ln_sq{i}"), [128, T], F32)) for i in range(2)]
        rows = pc.enter_context(env.nc.sbuf_tensor(uniq("ln_rows"), [1, 6 * T], F32))
        mbc = pc.enter_context(env.nc.sbuf_tensor(uniq("ln_mbc"), [128, T], F32))
        rbc = pc.enter_context(env.nc.sbuf_tensor(uniq("ln_rbc"), [128, T], F32))

        mm0, ve0, se0 = (snap.get(k, 0) for k in ("mm", "ve", "se"))
        mm_per_c = NT * (2 if not rms else 1)
        se_sq = [se0 + c + 1 for c in range(C8)]
        mm_stats = mm0 + C8 * mm_per_c
        n_ev = NT * (2 if not rms else 1)
        ve_rows = ve0 + n_ev + (4 if not rms else 1)
        mm_bc = mm_stats + (2 * NT if not rms else NT)

        @env.block.scalar
        def _(eng):
            env.fence(eng, snap)
            for c in range(C8):
                if c >= 2:
                    env.w(eng, "mm", mm0 + (c - 1) * mm_per_c)
                env.inc(eng.activation(sqb[c % 2][:], xsb[:, c * T:(c + 1) * T], AF.Square), "se")
            env.w(eng, "ve", ve_rows)
            eng.activation(rows[0:1, 4 * T:5 * T], rows[0:1, 3 * T:4 * T], AF.Ln,
                           bias=g["vt"][0:1, 0, eps_col:eps_col + 1])
            env.inc(eng.activation(rows[0:1, 5 * T:6 * T], rows[0:1, 4 * T:5 * T], AF.Exp,
                                   scale=-0.5), "se", 2)

        se_rstd = se0 + C8 + 2

        @env.block.tensor
        def _(eng):
            env.fence(eng, snap)
            for c in range(C8):
                env.w(eng, "se", se_sq[c])
                for t in range(NT):
                    if not rms:
                        eng.matmul(psA[t][0:1, :], g["onesk"][:], xsb[:, c * T + t * TT: c * T + (t + 1) * TT],
                                   start=(c == 0), stop=(c == C8 - 1))
                        env.inc(eng.matmul(psB[t][0:1, :], g["onesk"][:], sqb[c % 2][:, t * TT:(t + 1) * TT],
                                           start=(c == 0), stop=(c == C8 - 1)), "mm", 2)
                    else:
                        env.inc(eng.matmul(psB[t][0:1, :], g["onesk"][:], sqb[c % 2][:, t * TT:(t + 1) * TT],
                                           start=(c == 0), stop=(c == C8 - 1)), "mm")
            env.w(eng, "ve", ve_rows)
            env.w(eng, "se", se_rstd)
            for t in range(NT):
                if not rms:
                    eng.matmul(psA[t][:], g["onebc"][:], rows[0:1, 2 * T + t * TT: 2 * T + (t + 1) * TT],
                               start=True, stop=True)
                env.inc(eng.matmul(psB[t][:], g["onebc"][:], rows[0:1, 5 * T + t * TT: 5 * T + (t + 1) * TT],
                                   start=True, stop=True), "mm", 2 if not rms else 1)
            assert env.cnt["mm"] == mm_bc

        @env.block.vector
        def _(eng):
            env.fence(eng, snap)
            env.w(eng, "mm", mm_stats)
            for t in range(NT):
                if not rms:
                    eng.tensor_copy(rows[0:1, t * TT:(t + 1) * TT], psA[t][0:1, :])
                    env.inc(eng.tensor_copy(rows[0:1, T + t * TT: T + (t + 1) * TT], psB[t][0:1, :]), "ve", 2)
                else:
                    env.inc(eng.tensor_copy(rows[0:1, T + t * TT: T + (t + 1) * TT], psB[t][0:1, :]), "ve")
            if not rms:
                eng.tensor_scalar_mul(rows[0:1, 2 * T:3 * T], rows[0:1, 0:T], 1.0 / D)
                env.inc(eng.tensor_scalar_mul(rows[0:1, 3 * T:4 * T], rows[0:1, T:2 * T], 1.0 / D), "ve", 2)
                eng.tensor_mul(rows[0:1, 4 * T:5 * T], rows[0:1, 2 * T:3 * T], rows[0:1, 2 * T:3 * T])
                env.inc(eng.tensor_sub(rows[0:1, 3 * T:4 * T], rows[0:1, 3 * T:4 * T], rows[0:1, 4 * T:5 * T]), "ve", 2)
            else:
                env.inc(eng.tensor_scalar_mul(rows[0:1, 3 * T:4 * T], rows[0:1, T:2 * T], 1.0 / D), "ve")
            assert env.cnt["ve"] == ve_rows
            env.w(eng, "mm", mm_bc)
            for t in range(NT):
                if not rms:
                    eng.tensor_copy(mbc[:, t * TT:(t + 1) * TT], psA[t][:])
                env.inc(eng.tensor_copy(rbc[:, t * TT:(t + 1) * TT], psB[t][:]), "ve")
            for c in range(C8):
                xc = xsb[:, c * T:(c + 1) * T]
                if not rms:
                    eng.tensor_sub(xc, xc, mbc[:])
                    env.inc(eng.tensor_mul(xc, xc, rbc[:]), "ve")
                else:
                    if nw:
                        eng.tensor_mul(xc, xc, rbc[:])
                        env.inc(eng.tensor_scalar(xc, xc, g["vt"][:, c, V_NW:V_NW + 1], None, ALU.mult), "ve")
                    else:
                        env.inc(eng.tensor_mul(xc, xc, rbc[:]), "ve")
                ve_done.append(env.cnt["ve"])
    return ve_done


def emit_l0_proj(env, io, g, xn_fm, wb=None, wd_pre=None):
    snap = env.snap()
    with contextlib.ExitStack() as pc:
        E = pc.enter_context
        d16 = E(env.nc.sbuf_tensor(uniq("pj_d16"), [128, C8 * T], BF16))
        m16 = [E(env.nc.sbuf_tensor(uniq(f"pj_m{i}"), [128, C8, TT], BF16)) for i in range(2)]
        if wb is None:
            wb = [E(env.nc.sbuf_tensor(uniq(f"pj_w{i}"), [128, C8 * D], BF16)) for i in range(2)]
            wd_pre = []
        stg = [E(env.nc.sbuf_tensor(uniq(f"pj_stg{i}"), [128, C8, TT], F32)) for i in range(2)]
        stg16 = [E(env.nc.sbuf_tensor(uniq(f"pj_stg16_{i}"), [128, C8, TT], BF16)) for i in range(2)]
        pmm = [E(env.nc.psum_tensor(uniq(f"pj_p{i}"), [128, TT], F32)) for i in range(2)]

        NP = len(PROJ)
        mm0, ve0, se0, st0, gd0 = (snap.get(k, 0) for k in ("mm", "ve", "se", "st", "gd"))

        npre = len(wd_pre)
        wd_at = list(wd_pre) + [gd0 + 16 * 8 * (p + 1 - npre) for p in range(npre, NP)]
        ve_m, mm_g, mm_pt, se_e, ve_x, st_t = {}, {}, {}, {}, {}, {}
        ve = ve0 + C8 * 2
        mm, se, st = mm0, se0, st0
        for p in range(NP):
            for t in range(NT):
                ve += C8
                ve_m[(p, t)] = ve
                for o in range(C8):
                    mm += 1
                    mm_g[(p, t, o)] = mm
                    se += 1
                    se_e[(p, t, o)] = se
                    if p == 7:
                        ve += 1
                mm_pt[(p, t)] = mm
                if p == 7:
                    ve_x[(p, t)] = ve
                st += 16
                st_t[(p, t)] = st

        @env.block.gpsimd
        def _(eng):
            env.fence(eng, snap)
            for p in range(npre, NP):
                if p >= 2:
                    env.w(eng, "mm", mm_pt[(p - 2, NT - 1)])
                for c in range(C8):
                    env.inc(eng.dma_start(out=wb[p % 2][:, c * D:(c + 1) * D],
                                          in_=io["wp"][p, c * 128:(c + 1) * 128, :]), "gd", 16)
            assert env.cnt["gd"] == wd_at[-1]

        @env.block.vector
        def _(eng):
            env.fence(eng, snap)
            for c in range(C8):
                eng.tensor_sub(d16[:, c * T + 1:(c + 1) * T],
                               xn_fm[:, c * T:(c + 1) * T - 1],
                               xn_fm[:, c * T + 1:(c + 1) * T])
                env.inc(eng.tensor_scalar_mul(d16[:, c * T:c * T + 1], xn_fm[:, c * T:c * T + 1], -1.0), "ve", 2)
            for p in range(NP):
                _, vcol, one_minus, _, _, _, _ = PROJ[p]
                for t in range(NT):
                    pu = (p, t - 2) if t >= 2 else ((p - 1, t + 2) if p >= 1 else None)
                    if pu is not None:
                        env.w(eng, "mm", mm_pt[pu])
                    for c in range(C8):
                        coef = g["cm2"][:, c, vcol:vcol + 1] if one_minus else g["vt"][:, c, vcol:vcol + 1]
                        env.inc(eng.scalar_tensor_tensor(
                            m16[t % 2][:, c, :],
                            d16[:, c * T + t * TT: c * T + (t + 1) * TT],
                            coef,
                            xn_fm[:, c * T + t * TT: c * T + (t + 1) * TT],
                            ALU.mult, ALU.add), "ve")
                    assert env.cnt["ve"] == ve_m[(p, t)]
                    if p == 7:
                        for o in range(C8):
                            env.w(eng, "se", se_e[(p, t, o)])
                            env.inc(eng.tensor_mul(stg16[t % 2][:, o, :], stg[t % 2][:, o, :], stg[t % 2][:, o, :]), "ve")
                        assert env.cnt["ve"] == ve_x[(p, t)]

        @env.block.tensor
        def _(eng):
            env.fence(eng, snap)
            for p in range(NP):
                env.w(eng, "gd", wd_at[p])
                for t in range(NT):
                    env.w(eng, "ve", ve_m[(p, t)])
                    for o in range(C8):
                        gidx = (p * NT + t) * C8 + o
                        if gidx >= 2:
                            pv = gidx - 2
                            pp, rr = divmod(pv, NT * C8)
                            tt_, oo = divmod(rr, C8)
                            env.w(eng, "se", se_e[(pp, tt_, oo)])
                        for c in range(C8):
                            ins = eng.matmul(pmm[gidx % 2][:],
                                             wb[p % 2][:, c * D + o * 128: c * D + (o + 1) * 128],
                                             m16[t % 2][:, c, :],
                                             start=(c == 0), stop=(c == C8 - 1))
                        assert env.inc(ins, "mm") == mm_g[(p, t, o)]

        @env.block.scalar
        def _(eng):
            env.fence(eng, snap)
            for p in range(NP):
                _, _, _, fn, scale, bf16_out, _ = PROJ[p]
                for t in range(NT):
                    if (p * NT + t) >= 2:
                        pv = (p * NT + t) - 2
                        env.w(eng, "st", st_t[(pv // NT, pv % NT)])
                    for o in range(C8):
                        gidx = (p * NT + t) * C8 + o
                        env.w(eng, "mm", mm_g[(p, t, o)])
                        dst = stg16[t % 2][:, o, :] if bf16_out else stg[t % 2][:, o, :]
                        ins = eng.activation(dst, pmm[gidx % 2][:], fn, scale=scale)
                        assert env.inc(ins, "se") == se_e[(p, t, o)]

        @env.block.sync
        def _(eng):
            env.fence(eng, snap)
            for p in range(NP):
                bf16_out = PROJ[p][5]
                use16 = bf16_out or p == 7
                for t in range(NT):
                    if p == 7:
                        env.w(eng, "ve", ve_x[(p, t)])
                    else:
                        env.w(eng, "se", se_e[(p, t, C8 - 1)])
                    src = stg16[t % 2][:] if use16 else stg[t % 2][:]
                    ins = eng.dma_start(
                        out=io[PROJ[p][6]].rearrange("(o p) n -> p o n", p=128)[:, :, t * TT:(t + 1) * TT],
                        in_=src)
                    assert env.inc(ins, "st", 16) == st_t[(p, t)]


def emit_l0_scan(env, io, g, mout, wo_sb=None):
    snap = env.snap()
    with contextlib.ExitStack() as pc:
        E = pc.enter_context
        db = [E(env.nc.sbuf_tensor(uniq(f"sc_d{i}"), [128, T], F32)) for i in range(2)]
        ab = [E(env.nc.sbuf_tensor(uniq(f"sc_a{i}"), [128, T], F32)) for i in range(2)]
        kb = E(env.nc.sbuf_tensor(uniq("sc_k"), [128, T], F32))
        vb = E(env.nc.sbuf_tensor(uniq("sc_v"), [128, T], F32))
        rb = [E(env.nc.sbuf_tensor(uniq(f"sc_r{i}"), [128, T], F32)) for i in range(2)]
        gb = [E(env.nc.sbuf_tensor(uniq(f"sc_g{i}"), [128, T], BF16)) for i in range(2)]
        mx = [E(env.nc.sbuf_tensor(uniq(f"sc_m{i}"), [128, T], F32)) for i in range(2)]
        sqb = E(env.nc.sbuf_tensor(uniq("sc_q"), [128, T], F32))
        rows = E(env.nc.sbuf_tensor(uniq("sc_rows"), [2, 3 * T], F32))
        mbc = E(env.nc.sbuf_tensor(uniq("sc_mbc"), [128, T], F32))
        rbc = E(env.nc.sbuf_tensor(uniq("sc_rbc"), [128, T], F32))
        ps8 = [E(env.nc.psum_tensor(uniq(f"sc_ps{i}"), [128, TT], F32)) for i in range(8)]

        st0, ve0, se0, mm0, po0, gd0 = (snap.get(k, 0) for k in ("st", "ve", "se", "mm", "po", "gd"))
        st_ld = [st0 + 96 * (c + 1) for c in range(C8)]
        ve_u = [ve0 + 4 * c + 2 for c in range(C8)]
        ve_mx = [ve0 + 4 * c + 4 for c in range(C8)]
        po_var = [po0 + 5 * c + 1 for c in range(C8)]
        po_gn = [po0 + 5 * c + 5 for c in range(C8)]

        se_bc, se_sq, se_ev, se_msq, se_rstd = {}, {}, {}, {}, {}
        mm_bc, mm_st = {}, {}
        se = se0
        mm = mm0
        for s in range(10):
            if 2 <= s <= C8 + 1:
                se += 8
                se_bc[s - 2] = se
            if s < C8:
                se += 1
                se_sq[s] = se
            if 1 <= s <= C8:
                c = s - 1
                se += 8
                se_ev[c] = se
                se += 1
                se_msq[c] = se
                se += 2
                se_rstd[c] = se
            if 1 <= s <= C8:
                mm += 8
                mm_bc[s - 1] = mm
            if s < C8:
                mm += 8
                mm_st[s] = mm

        @env.block.sync
        def _(eng):
            env.fence(eng, snap)
            for c in range(C8):
                r = slice(c * 128, (c + 1) * 128)
                if c >= 2:
                    env.w(eng, "ve", ve_mx[c - 2])
                for nm, buf in (("d_decay", db), ("d_a", ab)):
                    env.inc(eng.dma_start(out=buf[c % 2][:], in_=io[nm][r, :]), "st", 16)
                if c >= 1:
                    env.w(eng, "ve", ve_u[c - 1])
                for nm, buf in (("d_ktanh", kb), ("d_v", vb)):
                    env.inc(eng.dma_start(out=buf[:], in_=io[nm][r, :]), "st", 16)
                env.inc(eng.dma_start(out=rb[c % 2][:], in_=io["d_r"][r, :]), "st", 16)
                if c >= 2:
                    env.w(eng, "po", po_gn[c - 2])
                env.inc(eng.dma_start(out=gb[c % 2][:], in_=io["d_g"][r, :]), "st", 16)
                assert env.cnt["st"] == st_ld[c]

        @env.block.vector
        def _(eng):
            env.fence(eng, snap)
            for c in range(C8):
                env.w(eng, "st", st_ld[c])
                eng.tensor_mul(ab[c % 2][:], ab[c % 2][:], kb[:])
                assert env.inc(eng.tensor_mul(ab[c % 2][:], ab[c % 2][:], vb[:]), "ve", 2) == ve_u[c]
                if c >= 2:
                    env.w(eng, "mm", mm_st[c - 2])
                    env.w(eng, "po", po_gn[c - 2])
                eng.tensor_tensor_scan(mx[c % 2][:], db[c % 2][:], ab[c % 2][:], 0.0, ALU.mult, ALU.add)
                assert env.inc(eng.tensor_mul(mx[c % 2][:], rb[c % 2][:], mx[c % 2][:]), "ve", 2) == ve_mx[c]

        @env.block.scalar
        def _(eng):
            env.fence(eng, snap)
            for s in range(10):
                if 2 <= s <= C8 + 1:
                    c = s - 2
                    env.w(eng, "mm", mm_bc[c])
                    if c >= 1:
                        env.w(eng, "po", po_gn[c - 1])
                    for t in range(NT):
                        eng.activation(mbc[:, t * TT:(t + 1) * TT], ps8[t][:], AF.Identity)
                        env.inc(eng.activation(rbc[:, t * TT:(t + 1) * TT], ps8[4 + t][:], AF.Identity),
                                "se", 2)
                    assert env.cnt["se"] == se_bc[c]
                if s < C8:
                    env.w(eng, "ve", ve_mx[s])
                    if s >= 1:
                        env.w(eng, "mm", mm_st[s - 1])
                    assert env.inc(eng.activation(sqb[:], mx[s % 2][:], AF.Square), "se") == se_sq[s]
                if 1 <= s <= C8:
                    c = s - 1
                    env.w(eng, "mm", mm_st[c])
                    for t in range(NT):
                        eng.activation(rows[:, t * TT:(t + 1) * TT], ps8[t][0:2, :], AF.Identity,
                                       scale=1.0 / 64.0)
                        env.inc(eng.activation(rows[:, T + t * TT: T + (t + 1) * TT], ps8[4 + t][0:2, :],
                                               AF.Identity, scale=1.0 / 64.0), "se", 2)
                    assert env.cnt["se"] == se_ev[c]
                    assert env.inc(eng.activation(rows[:, 2 * T:3 * T], rows[:, 0:T], AF.Square),
                                   "se") == se_msq[c]
                    env.w(eng, "po", po_var[c])
                    eng.activation(rows[:, 2 * T:3 * T], rows[:, T:2 * T], AF.Ln,
                                   bias=g["vt"][0:2, 0, V_EPSGN:V_EPSGN + 1])
                    assert env.inc(eng.activation(rows[:, 2 * T:3 * T], rows[:, 2 * T:3 * T], AF.Exp,
                                                  scale=-0.5), "se", 2) == se_rstd[c]

        @env.block.tensor
        def _(eng):
            env.fence(eng, snap)
            for s in range(10):
                if 1 <= s <= C8:
                    c = s - 1
                    env.w(eng, "se", se_rstd[c])
                    for t in range(NT):
                        eng.matmul(ps8[t][:], g["gsel"][:], rows[:, t * TT:(t + 1) * TT],
                                   start=True, stop=True)
                        env.inc(eng.matmul(ps8[4 + t][:], g["gsel"][:],
                                           rows[:, 2 * T + t * TT: 2 * T + (t + 1) * TT],
                                           start=True, stop=True), "mm", 2)
                    assert env.cnt["mm"] == mm_bc[c]
                if s < C8:
                    env.w(eng, "ve", ve_mx[s])
                    env.w(eng, "se", se_sq[s])
                    if s >= 1:
                        env.w(eng, "se", se_bc[s - 1])
                    for t in range(NT):
                        eng.matmul(ps8[t][0:2, :], g["gmask"][:], mx[s % 2][:, t * TT:(t + 1) * TT],
                                   start=True, stop=True)
                        env.inc(eng.matmul(ps8[4 + t][0:2, :], g["gmask"][:], sqb[:, t * TT:(t + 1) * TT],
                                           start=True, stop=True), "mm", 2)
                    assert env.cnt["mm"] == mm_st[s]

        @env.block.gpsimd
        def _(eng):
            env.fence(eng, snap)
            if wo_sb is not None:
                for c in range(C8):
                    env.inc(eng.dma_start(out=wo_sb[:, c * D:(c + 1) * D],
                                          in_=io["wout"][c * 128:(c + 1) * 128, :]), "gd", 16)
                env.pre_wout = env.cnt["gd"]
            for s in range(10):
                if 2 <= s <= C8 + 1:
                    c = s - 2
                    env.w(eng, "se", se_bc[c])
                    env.w(eng, "st", st_ld[c])
                    env.w(eng, "mm", mm_st[c])
                    n = env.inc(eng.tensor_sub(mx[c % 2][:], mx[c % 2][:], mbc[:]), "po")
                    eng.wait_ge(env.s("po"), n)
                    n = env.inc(eng.tensor_mul(mx[c % 2][:], mx[c % 2][:], rbc[:]), "po")
                    eng.wait_ge(env.s("po"), n)
                    n = env.inc(eng.tensor_scalar(mx[c % 2][:], mx[c % 2][:], g["vt"][:, c, V_GNW:V_GNW + 1],
                                                  g["vt"][:, c, V_GNB:V_GNB + 1], ALU.mult, ALU.add), "po")
                    eng.wait_ge(env.s("po"), n)
                    n = env.inc(eng.tensor_mul(mout[:, c * T:(c + 1) * T], mx[c % 2][:], gb[c % 2][:]), "po")
                    assert n == po_gn[c]
                    eng.wait_ge(env.s("po"), n)
                if 1 <= s <= C8:
                    c = s - 1
                    env.w(eng, "se", se_msq[c])
                    n = env.inc(eng.tensor_sub(rows[:, T:2 * T], rows[:, T:2 * T], rows[:, 2 * T:3 * T]),
                                "po")
                    assert n == po_var[c]
                    eng.wait_ge(env.s("po"), n)


def emit_l0_wout(env, io, g, mout, x1acc, wo, wo_gd, kact_sb=None, wfv_sb=None):
    snap = env.snap()
    with contextlib.ExitStack() as pc:
        E = pc.enter_context
        pmm = [E(env.nc.psum_tensor(uniq(f"wo_p{i}"), [128, TT], F32)) for i in range(2)]
        gd0, mm0, se0 = (snap.get(k, 0) for k in ("gd", "mm", "se"))
        wd = wo_gd
        mm_g = {}
        se_e = {}
        mm = mm0
        se = se0
        for t in range(NT):
            for o in range(C8):
                mm += 1
                mm_g[(t, o)] = mm
                se += 1
                se_e[(t, o)] = se

        @env.block.gpsimd
        def _(eng):
            env.fence(eng, snap)
            if wfv_sb is not None:
                for c in range(C8):
                    env.inc(eng.dma_start(out=wfv_sb[:, c * D:(c + 1) * D],
                                          in_=io["wfv"][c * 128:(c + 1) * 128, :]), "gd", 16)
            if kact_sb is not None:
                for c in range(C8):
                    env.inc(eng.dma_start(out=kact_sb[:, c * T:(c + 1) * T],
                                          in_=io["d_kact"][c * 128:(c + 1) * 128, :]), "gd", 16)
            env.pre_ffnv = env.cnt["gd"]

        @env.block.tensor
        def _(eng):
            env.fence(eng, snap)
            env.w(eng, "gd", wd)
            for t in range(NT):
                for o in range(C8):
                    gidx = t * C8 + o
                    if gidx >= 2:
                        pv = gidx - 2
                        env.w(eng, "se", se_e[(pv // C8, pv % C8)])
                    for c in range(C8):
                        ins = eng.matmul(pmm[gidx % 2][:],
                                         wo[:, c * D + o * 128: c * D + (o + 1) * 128],
                                         mout[:, c * T + t * TT: c * T + (t + 1) * TT],
                                         start=(c == 0), stop=(c == C8 - 1))
                    assert env.inc(ins, "mm") == mm_g[(t, o)]

        @env.block.scalar
        def _(eng):
            env.fence(eng, snap)
            for t in range(NT):
                for o in range(C8):
                    gidx = t * C8 + o
                    env.w(eng, "mm", mm_g[(t, o)])
                    ins = eng.activation(x1acc[:, o * T + t * TT: o * T + (t + 1) * TT],
                                         pmm[gidx % 2][:], AF.Identity,
                                         bias=g["vt"][:, o, V_BOUT:V_BOUT + 1])
                    assert env.inc(ins, "se") == se_e[(t, o)]


def emit_l0_ffnv(env, io, g, x1acc, x1_out, wf, ka, wd):
    snap = env.snap()
    with contextlib.ExitStack() as pc:
        E = pc.enter_context
        ostg = [E(env.nc.sbuf_tensor(uniq(f"fv_o{i}"), [128, TT], F32)) for i in range(2)]
        rfb = [E(env.nc.sbuf_tensor(uniq(f"fv_r{i}"), [128, TT], F32)) for i in range(2)]
        xfb = [E(env.nc.sbuf_tensor(uniq(f"fv_x{i}"), [128, T], F32)) for i in range(2)]
        pmm = [E(env.nc.psum_tensor(uniq(f"fv_p{i}"), [128, TT], F32)) for i in range(2)]
        gd0, mm0, se0, ve0, ax0, st0 = (snap.get(k, 0) for k in ("gd", "mm", "se", "ve", "ax", "st"))
        mm_g, se_e, ve_e, ax_l = {}, {}, {}, {}
        mm, se, ve, ax = mm0, se0, ve0, ax0
        for o in range(C8):
            for t in range(NT):
                mm += 1
                mm_g[(o, t)] = mm
                se += 1
                se_e[(o, t)] = se
                ax += 16
                ax_l[(o, t)] = ax
                ve += 2
                ve_e[(o, t)] = ve
        ax_x = {}
        ve_f = {}
        st_f = {}
        st = st0
        for c in range(C8):
            ax += 16
            ax_x[c] = ax
            ve += 1
            ve_f[c] = ve
            st += 16
            st_f[c] = st

        @env.block.tensor
        def _(eng):
            env.fence(eng, snap)
            env.w(eng, "gd", wd)
            for o in range(C8):
                for t in range(NT):
                    gidx = o * NT + t
                    if gidx >= 2:
                        pv = gidx - 2
                        env.w(eng, "se", se_e[(pv // NT, pv % NT)])
                    for c in range(C8):
                        ins = eng.matmul(pmm[gidx % 2][:],
                                         wf[:, c * D + o * 128: c * D + (o + 1) * 128],
                                         ka[:, c * T + t * TT: c * T + (t + 1) * TT],
                                         start=(c == 0), stop=(c == C8 - 1))
                    assert env.inc(ins, "mm") == mm_g[(o, t)]

        @env.block.scalar
        def _(eng):
            env.fence(eng, snap)
            for o in range(C8):
                for t in range(NT):
                    gidx = o * NT + t
                    if gidx >= 2:
                        pv = gidx - 2
                        env.w(eng, "ve", ve_e[(pv // NT, pv % NT)])
                    env.w(eng, "mm", mm_g[(o, t)])
                    ins = eng.activation(ostg[gidx % 2][:], pmm[gidx % 2][:], AF.Identity)
                    assert env.inc(ins, "se") == se_e[(o, t)]
                    ins = eng.dma_start(out=rfb[gidx % 2][:],
                                        in_=io["d_rf"][o * 128:(o + 1) * 128, t * TT:(t + 1) * TT])
                    ins.then_inc(env.s("ax"), 16)
                    env.cnt["ax"] += 16
            for c in range(C8):
                env.w(eng, "ve", ve_e[(C8 - 1, NT - 1)] if c == 0 else ve_f[c - 1])
                if c >= 2:
                    env.w(eng, "st", st_f[c - 2])
                ins = eng.dma_start(out=xfb[c % 2][:], in_=io["d_x1"][c * 128:(c + 1) * 128, :])
                ins.then_inc(env.s("ax"), 16)
                env.cnt["ax"] += 16

        @env.block.vector
        def _(eng):
            env.fence(eng, snap)
            for o in range(C8):
                for t in range(NT):
                    gidx = o * NT + t
                    env.w(eng, "se", se_e[(o, t)])
                    env.w(eng, "ax", ax_l[(o, t)])
                    eng.tensor_mul(ostg[gidx % 2][:], ostg[gidx % 2][:], rfb[gidx % 2][:])
                    dst = x1acc[:, o * T + t * TT: o * T + (t + 1) * TT]
                    env.inc(eng.tensor_add(dst, dst, ostg[gidx % 2][:]), "ve", 2)
            for c in range(C8):
                env.w(eng, "ax", ax_x[c])
                dst = x1acc[:, c * T:(c + 1) * T]
                assert env.inc(eng.tensor_add(dst, dst, xfb[c % 2][:]), "ve") == ve_f[c]

        @env.block.sync
        def _(eng):
            env.fence(eng, snap)
            for c in range(C8):
                env.w(eng, "ve", ve_f[c])
                ins = eng.dma_start(out=x1_out[c * 128:(c + 1) * 128, :], in_=x1acc[:, c * T:(c + 1) * T])
                assert env.inc(ins, "st", 16) == st_f[c]


def emit_l1_ln(env, io, g, xn16, xsb=None):
    snap = env.snap()
    with contextlib.ExitStack() as pc:
        E = pc.enter_context
        if xsb is None:
            x1sb = E(env.nc.sbuf_tensor(uniq("l1_x1"), [128, C8 * T], F32))
            ax0 = snap.get("ax", 0)

            @env.block.scalar
            def _(eng):
                env.fence(eng, snap)
                for c in range(C8):
                    ins = eng.dma_start(out=x1sb[:, c * T:(c + 1) * T],
                                        in_=io["d_x1"][c * 128:(c + 1) * 128, :])
                    ins.then_inc(env.s("ax"), 16)
                    env.cnt["ax"] += 16
            ld_done = env.cnt["ax"]

            @env.block.vector
            def _(eng):
                env.w(eng, "ax", ld_done)

            @env.block.tensor
            def _(eng):
                env.w(eng, "ax", ld_done)

            @env.block.scalar
            def _(eng):
                env.w(eng, "ax", ld_done)
        else:
            x1sb = xsb

        ve_done = emit_ln_fm(env, g, x1sb, V_EPSLN, rms=False, nw=False)

        @env.block.vector
        def _(eng):
            for c in range(C8):
                env.inc(eng.tensor_copy(xn16[:, c * T:(c + 1) * T], x1sb[:, c * T:(c + 1) * T]), "ve")
        ve16 = env.cnt["ve"]

        @env.block.sync
        def _(eng):
            env.fence(eng, snap)
            for c in range(C8):
                env.w(eng, "ve", ve_done[c])
                env.inc(eng.dma_start(out=io["d_z"][c * 128:(c + 1) * 128, :],
                                      in_=x1sb[:, c * T:(c + 1) * T]), "st", 16)
    return ve16


def emit_l1_attn(env, io, g, xn16):
    snap = env.snap()
    with contextlib.ExitStack() as pc:
        E = pc.enter_context
        qkv = E(env.nc.sbuf_tensor(uniq("at_qkv"), [128, C8, 3 * 64], BF16))
        wos = E(env.nc.sbuf_tensor(uniq("at_wo"), [64, D], BF16))
        q16 = E(env.nc.sbuf_tensor(uniq("at_q16"), [64, T], BF16))
        kf = E(env.nc.sbuf_tensor(uniq("at_kf"), [64, T], F32))
        vf = E(env.nc.sbuf_tensor(uniq("at_vf"), [64, T], F32))
        ktm = E(env.nc.sbuf_tensor(uniq("at_ktm"), [128, NTC * 64], BF16))
        vtm = E(env.nc.sbuf_tensor(uniq("at_vtm"), [128, NTC * 64], BF16))
        ef = E(env.nc.sbuf_tensor(uniq("at_ef"), [64, 64], F32))
        e16 = E(env.nc.sbuf_tensor(uniq("at_e16"), [64, 64], BF16))
        ctx16 = E(env.nc.sbuf_tensor(uniq("at_ctx"), [64, T], BF16))
        ostg = [E(env.nc.sbuf_tensor(uniq(f"at_o{i}"), [128, TT], F32)) for i in range(2)]
        xfb = [E(env.nc.sbuf_tensor(uniq(f"at_x{i}"), [128, TT], F32)) for i in range(2)]
        pq = [E(env.nc.psum_tensor(uniq(f"at_pq{i}"), [64, TT], F32)) for i in range(2)]
        ptr = [E(env.nc.psum_tensor(uniq(f"at_pt{i}"), [128, 64], F32)) for i in range(2)]
        pe = E(env.nc.psum_tensor(uniq("at_pe"), [64, 64], F32))
        po = [E(env.nc.psum_tensor(uniq(f"at_po{i}"), [128, TT], F32)) for i in range(2)]

        gd0, mm0, ve0, se0, ax0, st0 = (snap.get(k, 0) for k in ("gd", "mm", "ve", "se", "ax", "st"))
        wd = gd0 + 16 * 25
        mm_qkv = [mm0 + i + 1 for i in range(12)]
        se_qkv = [se0 + i + 1 for i in range(12)]
        mm_tp = [mm0 + 12 + i + 1 for i in range(32)]
        ve_tp = [ve0 + i + 1 for i in range(32)]
        mm_en = mm0 + 12 + 32 + 1
        ve_ef = ve0 + 32 + 1
        ve_e16 = ve0 + 32 + 2
        mm_cx = [mm_en + i + 1 for i in range(4)]
        se_cx = [se0 + 12 + i + 1 for i in range(4)]
        mm_ou = [mm_en + 4 + i + 1 for i in range(32)]
        se_ou = [se0 + 16 + i + 1 for i in range(32)]
        ax_ou = [ax0 + 16 * (i + 1) for i in range(32)]
        ve_ou = [ve_e16 + i + 1 for i in range(32)]
        st_ou = [st0 + 16 * (i + 1) for i in range(32)]

        @env.block.gpsimd
        def _(eng):
            env.fence(eng, snap)
            for j in range(3):
                for c in range(C8):
                    env.inc(eng.dma_start(out=qkv[:, c, j * 64:(j + 1) * 64],
                                          in_=io["wqkv"][j, c * 128:(c + 1) * 128, :]), "gd", 16)
            env.inc(eng.dma_start(out=wos[:], in_=io["wo"][:, :]), "gd", 16)

        @env.block.tensor
        def _(eng):
            env.fence(eng, snap)
            env.w(eng, "gd", wd)
            for j in range(3):
                for t in range(NT):
                    gi = j * NT + t
                    if gi >= 2:
                        env.w(eng, "se", se_qkv[gi - 2])
                    for c in range(C8):
                        ins = eng.matmul(pq[gi % 2][:], qkv[:, c, j * 64:(j + 1) * 64],
                                         xn16[:, c * T + t * TT: c * T + (t + 1) * TT],
                                         start=(c == 0), stop=(c == C8 - 1))
                    env.inc(ins, "mm")
            for j in range(2):
                src = kf if j == 0 else vf
                env.w(eng, "se", se_qkv[(j + 1) * NT + NT - 1])
                for tc in range(NTC):
                    gi = j * NTC + tc
                    if gi >= 2:
                        env.w(eng, "ve", ve_tp[gi - 2])
                    ins = eng.transpose(ptr[gi % 2][:], src[0:64, tc * TC:(tc + 1) * TC], g["idn"][0:64, 0:64])
                    env.inc(ins, "mm")
            env.w(eng, "ve", ve_tp[31])
            for tc in range(NTC):
                ins = eng.matmul(pe[:], ktm[:, tc * 64:(tc + 1) * 64], vtm[:, tc * 64:(tc + 1) * 64],
                                 start=(tc == 0), stop=(tc == NTC - 1))
            env.inc(ins, "mm")
            env.w(eng, "ve", ve_e16)
            for t in range(NT):
                if t >= 2:
                    env.w(eng, "se", se_cx[t - 2])
                ins = eng.matmul(pq[t % 2][:], e16[:], q16[:, t * TT:(t + 1) * TT], start=True, stop=True)
                env.inc(ins, "mm")
            for o in range(C8):
                for t in range(NT):
                    gi = o * NT + t
                    env.w(eng, "se", se_cx[t])
                    if gi >= 2:
                        env.w(eng, "se", se_ou[gi - 2])
                    ins = eng.matmul(po[gi % 2][:], wos[:, o * 128:(o + 1) * 128],
                                     ctx16[:, t * TT:(t + 1) * TT], start=True, stop=True)
                    env.inc(ins, "mm")

        @env.block.scalar
        def _(eng):
            env.fence(eng, snap)
            for j in range(3):
                for t in range(NT):
                    gi = j * NT + t
                    env.w(eng, "mm", mm_qkv[gi])
                    if j == 0:
                        dst = q16[:, t * TT:(t + 1) * TT]
                    elif j == 1:
                        dst = kf[0:64, t * TT:(t + 1) * TT]
                    else:
                        dst = vf[0:64, t * TT:(t + 1) * TT]
                    env.inc(eng.activation(dst, pq[gi % 2][0:64, :], AF.Identity), "se")
            for t in range(NT):
                env.w(eng, "mm", mm_cx[t])
                env.inc(eng.activation(ctx16[:, t * TT:(t + 1) * TT], pq[t % 2][0:64, :],
                                       AF.Identity, scale=CTX_SCALE), "se")
            for o in range(C8):
                for t in range(NT):
                    gi = o * NT + t
                    env.w(eng, "mm", mm_ou[gi])
                    if gi >= 2:
                        env.w(eng, "ve", ve_ou[gi - 2])
                        env.w(eng, "st", st_ou[gi - 2])
                    env.inc(eng.activation(ostg[gi % 2][:], po[gi % 2][:], AF.Identity), "se")
                    ins = eng.dma_start(out=xfb[gi % 2][:],
                                        in_=io["d_x1"][o * 128:(o + 1) * 128, t * TT:(t + 1) * TT])
                    ins.then_inc(env.s("ax"), 16)
                    env.cnt["ax"] += 16

        @env.block.vector
        def _(eng):
            env.fence(eng, snap)
            for j in range(2):
                dst = ktm if j == 0 else vtm
                for tc in range(NTC):
                    gi = j * NTC + tc
                    env.w(eng, "mm", mm_tp[gi])
                    env.inc(eng.tensor_copy(dst[:, tc * 64:(tc + 1) * 64], ptr[gi % 2][:]), "ve")
            env.w(eng, "mm", mm_en)
            env.inc(eng.tensor_copy(ef[:], pe[:]), "ve")
            env.inc(eng.tensor_copy(e16[:], ef[:]), "ve")
            for o in range(C8):
                for t in range(NT):
                    gi = o * NT + t
                    env.w(eng, "se", se_ou[gi])
                    env.w(eng, "ax", ax_ou[gi])
                    if gi >= 2:
                        env.w(eng, "st", st_ou[gi - 2])
                    env.inc(eng.tensor_add(ostg[gi % 2][:], ostg[gi % 2][:], xfb[gi % 2][:]), "ve")

        @env.block.sync
        def _(eng):
            env.fence(eng, snap)
            for o in range(C8):
                for t in range(NT):
                    gi = o * NT + t
                    env.w(eng, "ve", ve_ou[gi])
                    ins = eng.dma_start(out=io["d_acc"][o * 128:(o + 1) * 128, t * TT:(t + 1) * TT],
                                        in_=ostg[gi % 2][:])
                    assert env.inc(ins, "st", 16) == st_ou[gi]


def emit_l1_rms(env, io, g, lgt, zn16r, add_macc=False):
    snap = env.snap()
    with contextlib.ExitStack() as pc:
        E = pc.enter_context
        zsb = E(env.nc.sbuf_tensor(uniq("rm_z"), [128, C8 * T], F32))
        mbb = E(env.nc.sbuf_tensor(uniq("rm_mb"), [128, T], F32))
        sqb = E(env.nc.sbuf_tensor(uniq("rm_q"), [128, T], F32))

        rows = E(env.nc.sbuf_tensor(uniq("rm_rows"), [1, T], F32))
        rbc = E(env.nc.sbuf_tensor(uniq("rm_rbc"), [128, T], F32))
        wrt = E(env.nc.sbuf_tensor(uniq("rm_wrt"), [128, C8, 12], F32))
        lgf = E(env.nc.sbuf_tensor(uniq("rm_lgf"), [12, T], F32))
        psq = [E(env.nc.psum_tensor(uniq(f"rm_pq{i}"), [128, TT], F32)) for i in range(NT)]
        psr = [E(env.nc.psum_tensor(uniq(f"rm_pr{i}"), [128, TT], F32)) for i in range(NT)]

        gd0, mm0, ve0, se0, ax0, st0 = (snap.get(k, 0) for k in ("gd", "mm", "ve", "se", "ax", "st"))
        wd = gd0 + 16
        nld = 2 if add_macc else 1
        ax_z = [ax0 + 16 * nld * (c + 1) for c in range(C8)]
        nadd = 1 if add_macc else 0
        ve_add = [ve0 + nadd * (c + 1) for c in range(C8)]
        st_z = [st0 + 16 * (c + 1) for c in range(C8)] if add_macc else [0] * C8
        se_sq = [se0 + c + 1 for c in range(C8)]
        se_ev = se0 + 12
        se_rstd = se0 + 14
        se_rbc = se0 + 18
        se_lg = se0 + 22
        mm_st = [mm0 + 4 * (c + 1) for c in range(C8)]
        mm_bc = mm_st[-1] + 4
        mm_rt = [mm_bc + 4 * (c + 1) for c in range(C8)]
        mm_tp = [mm_rt[-1] + tc + 1 for tc in range(NTC)]
        vebase = ve0 + nadd * C8
        ve_mean = vebase + 1
        ve_zn = [vebase + 1 + 3 * c + 2 for c in range(C8)]
        ve_cast = [vebase + 1 + 3 * c + 3 for c in range(C8)]
        ve_tp = [ve_cast[-1] + tc + 1 for tc in range(NTC)]

        @env.block.gpsimd
        def _(eng):
            env.fence(eng, snap)
            env.inc(eng.dma_start(out=wrt[:], in_=io["wrt"].rearrange("(c p) v -> p c v", p=128)), "gd", 16)

        @env.block.scalar
        def _(eng):
            env.fence(eng, snap)
            for c in range(C8):
                ins = eng.dma_start(out=zsb[:, c * T:(c + 1) * T], in_=io["d_z"][c * 128:(c + 1) * 128, :])
                ins.then_inc(env.s("ax"), 16)
                env.cnt["ax"] += 16
                if add_macc:
                    if c >= 1:
                        env.w(eng, "ve", ve_add[c - 1])
                    ins = eng.dma_start(out=mbb[:], in_=io["d_macc"][:, c, :])
                    ins.then_inc(env.s("ax"), 16)
                    env.cnt["ax"] += 16
            for c in range(C8):
                if add_macc:
                    env.w(eng, "ve", ve_add[c])
                else:
                    env.w(eng, "ax", ax_z[c])
                if c >= 1:
                    env.w(eng, "mm", mm_st[c - 1])
                assert env.inc(eng.activation(sqb[:], zsb[:, c * T:(c + 1) * T], AF.Square), "se") == se_sq[c]
            env.w(eng, "mm", mm_st[-1])
            for t in range(NT):
                env.inc(eng.activation(rows[0:1, t * TT:(t + 1) * TT], psq[t][0:1, :], AF.Identity), "se")
            assert env.cnt["se"] == se_ev
            env.w(eng, "ve", ve_mean)
            eng.activation(rows[0:1, 0:T], rows[0:1, 0:T], AF.Ln,
                           bias=g["vt"][0:1, 0, V_EPSRMS:V_EPSRMS + 1])
            ins = eng.activation(rows[0:1, 0:T], rows[0:1, 0:T], AF.Exp, scale=-0.5)
            assert env.inc(ins, "se", 2) == se_rstd
            env.w(eng, "mm", mm_bc)
            for t in range(NT):
                env.inc(eng.activation(rbc[:, t * TT:(t + 1) * TT], psq[t][:], AF.Identity), "se")
            assert env.cnt["se"] == se_rbc
            env.w(eng, "mm", mm_rt[-1])
            for t in range(NT):
                env.inc(eng.activation(lgf[0:12, t * TT:(t + 1) * TT], psr[t][0:12, :], AF.Identity), "se")
            assert env.cnt["se"] == se_lg

        @env.block.vector
        def _(eng):
            env.fence(eng, snap)
            if add_macc:
                for c in range(C8):
                    env.w(eng, "ax", ax_z[c])
                    zc0 = zsb[:, c * T:(c + 1) * T]
                    assert env.inc(eng.tensor_add(zc0, zc0, mbb[:]), "ve") == ve_add[c]
            env.w(eng, "se", se_ev)
            assert env.inc(eng.tensor_scalar_mul(rows[0:1, 0:T], rows[0:1, 0:T], 1.0 / D), "ve") == ve_mean
            for c in range(C8):
                env.w(eng, "se", se_rbc)
                if add_macc:
                    env.w(eng, "st", st_z[c])
                zc = zsb[:, c * T:(c + 1) * T]
                eng.tensor_mul(zc, zc, rbc[:])
                env.inc(eng.tensor_scalar(zc, zc, g["vt"][:, c, V_NW:V_NW + 1], None, ALU.mult), "ve", 2)
                assert env.cnt["ve"] == ve_zn[c]
                assert env.inc(eng.tensor_copy(zn16r[:, c * T:(c + 1) * T], zc), "ve") == ve_cast[c]
            for tc in range(NTC):
                env.w(eng, "mm", mm_tp[tc])
                env.inc(eng.tensor_copy(lgt[:, tc, :], psq[tc % 2][0:128, 0:12]), "ve")
                assert env.cnt["ve"] == ve_tp[tc]

        @env.block.tensor
        def _(eng):
            env.fence(eng, snap)
            env.w(eng, "gd", wd)
            for c in range(C8):
                env.w(eng, "se", se_sq[c])
                for t in range(NT):
                    env.inc(eng.matmul(psq[t][0:1, :], g["onesk"][:], sqb[:, t * TT:(t + 1) * TT],
                                       start=(c == 0), stop=(c == C8 - 1)), "mm")
                assert env.cnt["mm"] == mm_st[c]
            env.w(eng, "ve", ve_mean)
            env.w(eng, "se", se_rstd)
            for t in range(NT):
                env.inc(eng.matmul(psq[t][:], g["onebc"][:], rows[0:1, t * TT:(t + 1) * TT],
                                   start=True, stop=True), "mm")
            assert env.cnt["mm"] == mm_bc
            for c in range(C8):
                env.w(eng, "ve", ve_zn[c])
                for t in range(NT):
                    env.inc(eng.matmul(psr[t][0:12, :], wrt[:, c, :], zsb[:, c * T + t * TT: c * T + (t + 1) * TT],
                                       start=(c == 0), stop=(c == C8 - 1)), "mm")
                assert env.cnt["mm"] == mm_rt[c]
            env.w(eng, "se", se_lg)
            env.w(eng, "se", se_rbc)
            for tc in range(NTC):
                if tc >= 2:
                    env.w(eng, "ve", ve_tp[tc - 2])
                ins = eng.transpose(psq[tc % 2][0:128, 0:12], lgf[0:12, tc * TC:(tc + 1) * TC],
                                    g["idn"][0:12, 0:12])
                env.inc(ins, "mm")
            assert env.cnt["mm"] == mm_tp[-1]

        if add_macc:
            @env.block.sync
            def _(eng):
                env.fence(eng, snap)
                for c in range(C8):
                    env.w(eng, "ve", ve_add[c])
                    assert env.inc(eng.dma_start(out=io["d_z"][c * 128:(c + 1) * 128, :],
                                                 in_=zsb[:, c * T:(c + 1) * T]), "st", 16) == st_z[c]

    return


def emit_l1_router(env, io, g, lgt, cmb_fm):
    snap = env.snap()
    from concourse.bass import broadcast_tensor_aps

    def bc(small, big):
        a, _ = broadcast_tensor_aps(small, big)
        return a

    with contextlib.ExitStack() as pc:
        E = pc.enter_context
        rw = E(env.nc.sbuf_tensor(uniq("rt_rw"), [128, NTC, 40], F32))
        cmbt = E(env.nc.sbuf_tensor(uniq("rt_cmbt"), [128, NTC, 8], F32))
        ptp = [E(env.nc.psum_tensor(uniq(f"rt_p{i}"), [8, 128], F32)) for i in range(2)]

        mm0, ve0, se0 = (snap.get(k, 0) for k in ("mm", "ve", "se"))

        @env.block.vector
        def _(eng):
            env.fence(eng, snap)

            def barrier():
                eng.wait_ge(env.s("ve"), env.cnt["ve"])

            lg4 = lgt[:, :, 0:4]
            env.inc(eng.reduce_max(rw[:, :, 0:1], lg4, AX), "ve")
            barrier()
            env.inc(eng.tensor_tensor(rw[:, :, 1:5], lg4, bc(rw[:, :, 0:1], lg4), ALU.subtract), "ve")
            env.w(eng, "se", se0 + 1)
            env.inc(eng.tensor_reduce(rw[:, :, 5:6], rw[:, :, 1:5], AX, ALU.add), "ve")
            barrier()
            env.inc(eng.reciprocal(rw[:, :, 6:7], rw[:, :, 5:6]), "ve")
            barrier()
            p = rw[:, :, 8:12]
            env.inc(eng.tensor_tensor(p, rw[:, :, 1:5], bc(rw[:, :, 6:7], p), ALU.mult), "ve")
            env.inc(eng.reduce_max(rw[:, :, 12:13], p, AX), "ve")
            barrier()
            env.inc(eng.tensor_tensor(rw[:, :, 13:17], p, bc(rw[:, :, 12:13], p), ALU.is_ge), "ve")
            env.inc(eng.scalar_tensor_tensor(rw[:, :, 17:21], rw[:, :, 13:17], -2.0, p,
                                             ALU.mult, ALU.add), "ve")
            env.inc(eng.reduce_max(rw[:, :, 21:22], rw[:, :, 17:21], AX), "ve")
            barrier()
            env.inc(eng.tensor_tensor(rw[:, :, 22:26], p, bc(rw[:, :, 21:22], p), ALU.is_ge), "ve")
            env.inc(eng.tensor_add(rw[:, :, 26:27], rw[:, :, 12:13], rw[:, :, 21:22]), "ve")
            barrier()
            env.inc(eng.reciprocal(rw[:, :, 27:28], rw[:, :, 26:27]), "ve")
            barrier()
            G = rw[:, :, 28:32]
            env.inc(eng.tensor_tensor(G, p, bc(rw[:, :, 27:28], G), ALU.mult), "ve")
            env.inc(eng.tensor_mul(G, G, rw[:, :, 22:26]), "ve")
            env.inc(eng.tensor_tensor(rw[:, :, 32:36], lgt[:, :, 5:12:2], lgt[:, :, 4:12:2], ALU.is_gt), "ve")
            env.inc(eng.tensor_scalar(rw[:, :, 36:40], rw[:, :, 32:36], -1.0, 1.0, ALU.mult, ALU.add), "ve")
            env.inc(eng.tensor_mul(cmbt[:, :, 0:8:2], G, rw[:, :, 36:40]), "ve")
            env.inc(eng.tensor_mul(cmbt[:, :, 1:8:2], G, rw[:, :, 32:36]), "ve")
        ve_cmb = env.cnt["ve"]

        @env.block.scalar
        def _(eng):
            env.fence(eng, snap)
            env.w(eng, "ve", ve0 + 2)
            env.inc(eng.activation(rw[:, :, 1:5], rw[:, :, 1:5], AF.Exp), "se")

        @env.block.tensor
        def _(eng):
            env.fence(eng, snap)
            env.w(eng, "ve", ve_cmb)
            for tc in range(NTC):
                if tc >= 2:
                    env.w(eng, "ve", ve_cmb + tc - 1)
                env.inc(eng.transpose(ptp[tc % 2][:], cmbt[:, tc, :], g["idn"][:]), "mm")

        @env.block.vector
        def _(eng):
            for tc in range(NTC):
                env.w(eng, "mm", mm0 + tc + 1)
                env.inc(eng.tensor_copy(cmb_fm[0:8, tc * TC:(tc + 1) * TC], ptp[tc % 2][:]), "ve")
    return env.cnt["ve"]


def emit_l1_experts(env, io, g, cmb_fm, zn16r, loop_idx, w1b0=None, wd0=None):
    snap = env.snap()
    with contextlib.ExitStack() as pc:
        E = pc.enter_context
        w1b = [w1b0 if (i == 0 and w1b0 is not None) else
               E(env.nc.sbuf_tensor(uniq(f"ex_w1{i}"), [128, C8 * 2048], BF16)) for i in range(2)]
        w2b = [E(env.nc.sbuf_tensor(uniq(f"ex_w2{i}"), [128, NF * D], BF16)) for i in range(2)]

        ht = E(env.nc.sbuf_tensor(uniq("ex_ht"), [128, NF * TT], BF16))
        yst = E(env.nc.sbuf_tensor(uniq("ex_yst"), [128, C8, TT], F32))
        cbb = [E(env.nc.sbuf_tensor(uniq(f"ex_cb{i}"), [128, TT], F32)) for i in range(2)]
        pcb = [E(env.nc.psum_tensor(uniq(f"ex_pcb{i}"), [128, TT], F32)) for i in range(2)]
        ph = [E(env.nc.psum_tensor(uniq(f"ex_ph{i}"), [128, TT], F32)) for i in range(3)]
        py = [E(env.nc.psum_tensor(uniq(f"ex_py{i}"), [128, TT], F32)) for i in range(3)]

        gd0, mm0, ve0, se0, st0 = (snap.get(k, 0) for k in ("gd", "mm", "ve", "se", "st"))

        mm_cb, mm_h, mm_he, mm_y, mm_e = {}, {}, {}, {}, {}
        se_h, se_he = {}, {}
        ve_cb, ve_y, ve_ye = {}, {}, {}
        gd_w, gd_w2, gd_mc = {}, {}, {}
        st_zn = {}
        mm = mm0
        se = se0
        ve = ve0
        gd = gd0
        st = st0
        for e in range(NEXP):
            if e < 2:
                if e == 0 and wd0 is not None:
                    gd_w[0] = wd0
                else:
                    gd += 16 * 8
                    gd_w[e] = gd
                gd += 16 * 16
                gd_w2[e] = gd
        for e in range(NEXP):
            for t in range(NT):
                mm += 1
                mm_cb[(e, t)] = mm
                for f in range(NF):
                    mm += 1
                    mm_h[(e, t, f)] = mm
                    se += 1
                    se_h[(e, t, f)] = se
                mm_he[(e, t)] = mm
                se_he[(e, t)] = se
                ve += 1
                ve_cb[(e, t)] = ve
                for d in range(C8):
                    mm += 1
                    mm_y[(e, t, d)] = mm
                    ve += 1
                    ve_y[(e, t, d)] = ve
                ve_ye[(e, t)] = ve
                gd += 16
                gd_mc[(e, t)] = gd
            mm_e[e] = mm
            if e + 2 < NEXP:
                gd += 16 * 8
                gd_w[e + 2] = gd
                gd += 16 * 16
                gd_w2[e + 2] = gd

        @env.block.gpsimd
        def _(eng):
            env.fence(eng, snap)
            for e in range(2):
                if not (e == 0 and wd0 is not None):
                    for c in range(C8):
                        env.inc(eng.dma_start(out=w1b[e % 2][:, c * 2048:(c + 1) * 2048],
                                              in_=io["w1"][e, c * 128:(c + 1) * 128, :]), "gd", 16)
                    assert env.cnt["gd"] == gd_w[e]
                for f in range(NF):
                    env.inc(eng.dma_start(out=w2b[e % 2][:, f * D:(f + 1) * D],
                                          in_=io["w2"][e, f * 128:(f + 1) * 128, :]), "gd", 16)
                assert env.cnt["gd"] == gd_w2[e]
            for e in range(NEXP):
                for t in range(NT):
                    env.w(eng, "ve", ve_ye[(e, t)])
                    src = yst[:]
                    dst = io["d_macc"][:, :, t * TT:(t + 1) * TT]
                    if e == 0:
                        ins = eng.dma_start(out=dst, in_=src)
                    else:
                        ins = eng.dma_start(out=dst, in_=src, accum_op=ALU.add)
                    assert env.inc(ins, "gd", 16) == gd_mc[(e, t)]
                if e + 2 < NEXP:
                    env.w(eng, "mm", mm_e[e])
                    for c in range(C8):
                        env.inc(eng.dma_start(out=w1b[e % 2][:, c * 2048:(c + 1) * 2048],
                                              in_=io["w1"][e + 2, c * 128:(c + 1) * 128, :]), "gd", 16)
                    assert env.cnt["gd"] == gd_w[e + 2]
                    for f in range(NF):
                        env.inc(eng.dma_start(out=w2b[e % 2][:, f * D:(f + 1) * D],
                                              in_=io["w2"][e + 2, f * 128:(f + 1) * 128, :]), "gd", 16)
                    assert env.cnt["gd"] == gd_w2[e + 2]

        @env.block.tensor
        def _(eng):
            env.fence(eng, snap)
            for e in range(NEXP):
                env.w(eng, "gd", gd_w[e])
                for t in range(NT):
                    pu = (e, t - 2) if t >= 2 else ((e - 1, t + 2) if e >= 1 else None)
                    if pu is not None:
                        env.w(eng, "ve", ve_cb[pu])
                    env.inc(eng.matmul(pcb[t % 2][:], g["oneh"][:, e * 128:(e + 1) * 128],
                                       cmb_fm[0:8, t * TT:(t + 1) * TT], start=True, stop=True), "mm")
                    for f in range(NF):
                        gi = (e * NT + t) * NF + f
                        if gi >= 3:
                            pv = gi - 3
                            pe_, rr = divmod(pv, NT * NF)
                            pt_, pf = divmod(rr, NF)
                            env.w(eng, "se", se_h[(pe_, pt_, pf)])
                        for c in range(C8):
                            ins = eng.matmul(ph[gi % 3][:],
                                             w1b[e % 2][:, c * 2048 + f * 128: c * 2048 + (f + 1) * 128],
                                             zn16r[:, c * T + t * TT: c * T + (t + 1) * TT],
                                             start=(c == 0), stop=(c == C8 - 1))
                        assert env.inc(ins, "mm") == mm_h[(e, t, f)]
                    env.w(eng, "se", se_he[(e, t)])
                    if t == 0:
                        env.w(eng, "gd", gd_w2[e])
                    for d in range(C8):
                        gi = (e * NT + t) * C8 + d
                        if gi >= 3:
                            pv = gi - 3
                            pe_, rr = divmod(pv, NT * C8)
                            pt_, pd = divmod(rr, C8)
                            env.w(eng, "ve", ve_y[(pe_, pt_, pd)])
                        for f in range(NF):
                            ins = eng.matmul(py[gi % 3][:],
                                             w2b[e % 2][:, f * D + d * 128: f * D + (d + 1) * 128],
                                             ht[:, f * TT:(f + 1) * TT],
                                             start=(f == 0), stop=(f == NF - 1))
                        assert env.inc(ins, "mm") == mm_y[(e, t, d)]

        @env.block.scalar
        def _(eng):
            env.fence(eng, snap)
            for e in range(NEXP):
                for t in range(NT):
                    for f in range(NF):
                        env.w(eng, "mm", mm_h[(e, t, f)])
                        ins = eng.activation(ht[:, f * TT:(f + 1) * TT], ph[((e * NT + t) * NF + f) % 3][:],
                                             AF.Silu)
                        assert env.inc(ins, "se") == se_h[(e, t, f)]

        @env.block.vector
        def _(eng):
            env.fence(eng, snap)
            for e in range(NEXP):
                for t in range(NT):
                    env.w(eng, "mm", mm_cb[(e, t)])
                    pu = (e, t - 2) if t >= 2 else ((e - 1, t + 2) if e >= 1 else None)
                    if pu is not None:
                        env.w(eng, "ve", ve_ye[pu])
                    assert env.inc(eng.tensor_copy(cbb[t % 2][:], pcb[t % 2][:]), "ve") == ve_cb[(e, t)]
                    for d in range(C8):
                        env.w(eng, "mm", mm_y[(e, t, d)])
                        if d == 0:
                            pu2 = (e, t - 1) if t >= 1 else ((e - 1, NT - 1) if e >= 1 else None)
                            if pu2 is not None:
                                env.w(eng, "gd", gd_mc[pu2])
                        ins = eng.tensor_tensor(yst[:, d, :], py[((e * NT + t) * C8 + d) % 3][:],
                                                cbb[t % 2][:], ALU.mult)
                        assert env.inc(ins, "ve") == ve_y[(e, t, d)]
    return


def emit_l1_zupd(env, io, g, last, fs_sb):
    assert last
    snap = env.snap()
    with contextlib.ExitStack() as pc:
        E = pc.enter_context
        zb = [E(env.nc.sbuf_tensor(uniq(f"zu_z{i}"), [128, T], F32)) for i in range(2)]
        mb = [E(env.nc.sbuf_tensor(uniq(f"zu_m{i}"), [128, T], F32)) for i in range(2)]
        ab = [E(env.nc.sbuf_tensor(uniq(f"zu_a{i}"), [128, T], F32)) for i in range(2)]
        ax0, ve0, st0, gd0 = (snap.get(k, 0) for k in ("ax", "ve", "st", "gd"))
        ax_at = [ax0 + 16 * (c + 1) for c in range(C8)]
        st_at = [st0 + 16 * (c + 1) for c in range(C8)]
        gd_at = [gd0 + 16 * (c + 1) for c in range(C8)]
        ve_at = [ve0 + 2 * (c + 1) for c in range(C8)]

        @env.block.scalar
        def _(eng):
            env.fence(eng, snap)
            for c in range(C8):
                if c >= 2:
                    env.w(eng, "ve", ve_at[c - 2])
                ins = eng.dma_start(out=zb[c % 2][:], in_=io["d_z"][c * 128:(c + 1) * 128, :])
                ins.then_inc(env.s("ax"), 16)
                env.cnt["ax"] += 16

        @env.block.sync
        def _(eng):
            env.fence(eng, snap)
            for c in range(C8):
                if c >= 2:
                    env.w(eng, "ve", ve_at[c - 2])
                env.inc(eng.dma_start(out=mb[c % 2][:], in_=io["d_macc"][:, c, :]), "st", 16)

        @env.block.gpsimd
        def _(eng):
            env.fence(eng, snap)
            for c in range(C8):
                if c >= 2:
                    env.w(eng, "ve", ve_at[c - 2])
                env.inc(eng.dma_start(out=ab[c % 2][:], in_=io["d_acc"][c * 128:(c + 1) * 128, :]), "gd", 16)

        @env.block.vector
        def _(eng):
            env.fence(eng, snap)
            for c in range(C8):
                env.w(eng, "ax", ax_at[c])
                env.w(eng, "st", st_at[c])
                env.w(eng, "gd", gd_at[c])
                env.inc(eng.tensor_add(zb[c % 2][:], zb[c % 2][:], mb[c % 2][:]), "ve")
                env.inc(eng.tensor_add(fs_sb[:, c * T:(c + 1) * T], zb[c % 2][:], ab[c % 2][:]), "ve")


def emit_fin(env, io, g, fs_sb):
    snap = env.snap()
    with contextlib.ExitStack() as pc:
        E = pc.enter_context
        otm = [E(env.nc.sbuf_tensor(uniq(f"fn_o{i}"), [128, D], F32)) for i in range(2)]
        ptf = [E(env.nc.psum_tensor(uniq(f"fn_p{i}"), [128, 128], F32)) for i in range(4)]
        mm0, ve0, se0, st0 = (snap.get(k, 0) for k in ("mm", "ve", "se", "st"))
        mm_tp = {}
        ve_ev = {}
        se_ev = {}
        st_at = [st0 + 16 * (i + 1) for i in range(NTC)]
        mm = mm0
        ve = ve0
        se = se0
        for tc in range(NTC):
            for c in range(C8):
                mm += 1
                mm_tp[(tc, c)] = mm
                if c % 2 == 0:
                    ve += 1
                    ve_ev[(tc, c)] = ve
                else:
                    se += 1
                    se_ev[(tc, c)] = se

        @env.block.tensor
        def _(eng):
            env.fence(eng, snap)
            for tc in range(NTC):
                for c in range(C8):
                    gi = tc * C8 + c
                    if gi >= 4:
                        pv = gi - 4
                        ptc, pcb = divmod(pv, C8)
                        if pcb % 2 == 0:
                            env.w(eng, "ve", ve_ev[(ptc, pcb)])
                        else:
                            env.w(eng, "se", se_ev[(ptc, pcb)])
                    ins = eng.transpose(ptf[gi % 4][:], fs_sb[:, c * T + tc * TC: c * T + (tc + 1) * TC],
                                        g["idn"][:])
                    assert env.inc(ins, "mm") == mm_tp[(tc, c)]

        @env.block.vector
        def _(eng):
            env.fence(eng, snap)
            for tc in range(NTC):
                if tc >= 2:
                    env.w(eng, "st", st_at[tc - 2])
                for c in range(0, C8, 2):
                    env.w(eng, "mm", mm_tp[(tc, c)])
                    gi = tc * C8 + c
                    assert env.inc(eng.tensor_copy(otm[tc % 2][:, c * 128:(c + 1) * 128], ptf[gi % 4][:]),
                                   "ve") == ve_ev[(tc, c)]

        @env.block.scalar
        def _(eng):
            env.fence(eng, snap)
            for tc in range(NTC):
                if tc >= 2:
                    env.w(eng, "st", st_at[tc - 2])
                for c in range(1, C8, 2):
                    env.w(eng, "mm", mm_tp[(tc, c)])
                    gi = tc * C8 + c
                    assert env.inc(eng.activation(otm[tc % 2][:, c * 128:(c + 1) * 128], ptf[gi % 4][:],
                                                  AF.Identity), "se") == se_ev[(tc, c)]

        @env.block.sync
        def _(eng):
            env.fence(eng, snap)
            for tc in range(NTC):
                for c in range(0, C8, 2):
                    env.w(eng, "ve", ve_ev[(tc, c)])
                for c in range(1, C8, 2):
                    env.w(eng, "se", se_ev[(tc, c)])
                ins = eng.dma_start(out=io["out"][tc * TC:(tc + 1) * TC, :], in_=otm[tc % 2][:])
                assert env.inc(ins, "st", 16) == st_at[tc]
            env.w(eng, "st", st_at[-1])


def build(mode):
    nc = bass.Bass("TRN2", target_bir_lowering=False, debug=False)
    io = {}

    def ei(name, shape, dtype=F32):
        io[name] = nc.dram_tensor(name, shape, dtype, kind="ExternalInput").ap()

    def eo(name, shape, dtype=F32):
        io[name] = nc.dram_tensor(name, shape, dtype, kind="ExternalOutput").ap()

    def itn(name, shape, dtype=F32):
        io[name] = nc.dram_tensor(name, shape, dtype, kind="Internal").ap()

    ei("vecs", [D, NVEC])
    ei("idn", [128, 128])
    ei("gmask", [128, 2])
    ei("gsel", [2, 128])
    ei("oneh", [8, 8 * 128])
    ei("onesk", [128, 1])
    ei("onebc", [1, 128])

    has_l0 = mode in ("l0", "all")
    has_l1 = mode in ("l1", "all")

    if has_l0:
        ei("x", [T, D])
        ei("wp", [8, D, D])
        ei("wout", [D, D])
        ei("wfv", [D, D])
        mk = eo if DEBUG_L0 else itn
        for n in ("d_decay", "d_ktanh", "d_v", "d_a", "d_r", "d_rf"):
            mk(n, [D, T])
        mk("d_g", [D, T], BF16)
        mk("d_kact", [D, T], BF16)
        if DEBUG_L0:
            eo("d_xn", [D, T])
            eo("d_mout", [D, T], BF16)
    if has_l1:
        ei("wqkv", [3, D, 64])
        ei("wo", [64, D])
        ei("wrt", [D, 12])
        ei("w1", [NEXP, D, 2048])
        ei("w2", [NEXP, 2048, D])
        itn("d_z", [D, T])
        itn("d_macc", [128, C8, T])
        itn("d_acc", [D, T])
        eo("out", [T, D])
    if mode == "l0":
        eo("x1", [D, T])
        io["d_x1"] = io["x1"]
    elif mode == "l1":
        ei("x1", [D, T])
        io["d_x1"] = io["x1"]
    else:
        itn("d_x1", [D, T])

    with contextlib.ExitStack() as ctx:
        block = ctx.enter_context(nc.Block())
        env = Env(nc, ctx, block)
        for s in FENCE_SEMS:
            env.s(s)
        g = emit_setup(env, io)

        if has_l0:
            with contextlib.ExitStack() as l0ctx:
                xsb = l0ctx.enter_context(nc.sbuf_tensor(uniq("l0_x"), [128, C8 * T], F32))
                wb = [l0ctx.enter_context(nc.sbuf_tensor(uniq(f"l0_wb{i}"), [128, C8 * D], BF16))
                      for i in range(2)]
                emit_tp_in(env, io, g, xsb)
                wd_pre = []

                @env.block.gpsimd
                def _(eng):
                    for p in range(2):
                        for c in range(C8):
                            env.inc(eng.dma_start(out=wb[p][:, c * D:(c + 1) * D],
                                                  in_=io["wp"][p, c * 128:(c + 1) * 128, :]), "gd", 16)
                        wd_pre.append(env.cnt["gd"])
                emit_ln_fm(env, g, xsb, V_EPSLN, rms=False)
                if DEBUG_L0:
                    snap_dbg = env.snap()

                    @env.block.sync
                    def _(eng):
                        env.fence(eng, snap_dbg)
                        for c in range(C8):
                            env.inc(eng.dma_start(out=io["d_xn"][c * 128:(c + 1) * 128, :],
                                                  in_=xsb[:, c * T:(c + 1) * T]), "st", 16)
                        env.w(eng, "st", env.cnt["st"])
                emit_l0_proj(env, io, g, xsb, wb=wb, wd_pre=wd_pre)
            with contextlib.ExitStack() as l0ctx2:
                mout = l0ctx2.enter_context(nc.sbuf_tensor(uniq("l0_mout"), [128, C8 * T], BF16))
                wo_sb = l0ctx2.enter_context(nc.sbuf_tensor(uniq("l0_wo"), [128, C8 * D], BF16))
                emit_l0_scan(env, io, g, mout, wo_sb=wo_sb)
                if DEBUG_L0:
                    snap_dbg2 = env.snap()

                    @env.block.sync
                    def _(eng):
                        env.fence(eng, snap_dbg2)
                        for c in range(C8):
                            env.inc(eng.dma_start(out=io["d_mout"][c * 128:(c + 1) * 128, :],
                                                  in_=mout[:, c * T:(c + 1) * T]), "st", 16)
                        env.w(eng, "st", env.cnt["st"])
                x1acc = l0ctx2.enter_context(nc.sbuf_tensor(uniq("l0_x1a"), [128, C8 * T], F32))
                wfv_sb = l0ctx2.enter_context(nc.sbuf_tensor(uniq("l0_wfv"), [128, C8 * D], BF16))
                kact_sb = l0ctx2.enter_context(nc.sbuf_tensor(uniq("l0_ka"), [128, C8 * T], BF16))
                emit_l0_wout(env, io, g, mout, x1acc, wo_sb, env.pre_wout,
                             kact_sb=kact_sb, wfv_sb=wfv_sb)
                emit_l0_ffnv(env, io, g, x1acc, io["d_x1"], wfv_sb, kact_sb, env.pre_ffnv)
            if mode == "l0":
                snap = env.snap()

                @env.block.sync
                def _(eng):
                    env.fence(eng, snap)

        if has_l1:
            with contextlib.ExitStack() as l1ctx:
                xn16 = l1ctx.enter_context(nc.sbuf_tensor(uniq("l1_xn16"), [128, C8 * T], BF16))
                emit_l1_ln(env, io, g, xn16)
                emit_l1_attn(env, io, g, xn16)
            with contextlib.ExitStack() as l1ctx2:
                zn16r = l1ctx2.enter_context(nc.sbuf_tensor(uniq("l1_zn16"), [128, C8 * T], BF16))
                lgt = l1ctx2.enter_context(nc.sbuf_tensor(uniq("l1_lgt"), [128, NTC, 12], F32))
                cmb = l1ctx2.enter_context(nc.sbuf_tensor(uniq("l1_cmb"), [8, T], BF16))
                w1b0 = l1ctx2.enter_context(nc.sbuf_tensor(uniq("l1_w10"), [128, C8 * 2048], BF16))
                for l in range(2):
                    mm_before = env.cnt["mm"]
                    snap_pf = env.snap()

                    @env.block.gpsimd
                    def _(eng):
                        env.fence(eng, snap_pf)
                        for c in range(C8):
                            env.inc(eng.dma_start(out=w1b0[:, c * 2048:(c + 1) * 2048],
                                                  in_=io["w1"][0, c * 128:(c + 1) * 128, :]), "gd", 16)
                    wd0 = env.cnt["gd"]
                    emit_l1_rms(env, io, g, lgt, zn16r, add_macc=(l == 1))
                    emit_l1_router(env, io, g, lgt, cmb)
                    emit_l1_experts(env, io, g, cmb, zn16r, l, w1b0=w1b0, wd0=wd0)
                fs_sb = l1ctx2.enter_context(nc.sbuf_tensor(uniq("l1_fs"), [128, C8 * T], F32))
                emit_l1_zupd(env, io, g, last=True, fs_sb=fs_sb)
                emit_fin(env, io, g, fs_sb)
    return nc


def _host_prep(inputs):
    f = np.float32
    t = lambda a: np.ascontiguousarray(np.asarray(a, dtype=f).T)
    wp = np.stack([
        t(inputs["L0_attn_W_w"]), t(inputs["L0_attn_W_k"]), t(inputs["L0_attn_W_v"]),
        t(inputs["L0_attn_W_a"]), t(inputs["L0_attn_W_r"]), t(inputs["L0_attn_W_g"]),
        t(inputs["L0_ffn_W_r"]), t(inputs["L0_ffn_W_k"]),
    ])
    vecs = np.zeros((D, NVEC), f)
    vecs[:, V_XW] = np.asarray(inputs["L0_attn_x_w"], f).reshape(-1)
    vecs[:, V_XK] = np.asarray(inputs["L0_attn_x_k"], f).reshape(-1)
    vecs[:, V_XV] = np.asarray(inputs["L0_attn_x_v"], f).reshape(-1)
    vecs[:, V_XA] = np.asarray(inputs["L0_attn_x_a"], f).reshape(-1)
    vecs[:, V_XR] = np.asarray(inputs["L0_attn_x_r"], f).reshape(-1)
    vecs[:, V_XG] = np.asarray(inputs["L0_attn_x_g"], f).reshape(-1)
    vecs[:, V_MUR] = np.asarray(inputs["L0_ffn_mu_r"], f).reshape(-1)
    vecs[:, V_MUK] = np.asarray(inputs["L0_ffn_mu_k"], f).reshape(-1)
    vecs[:, V_BOUT] = np.asarray(inputs["L0_attn_b_out"], f).reshape(-1)
    vecs[:, V_GNW] = np.asarray(inputs["L0_attn_gn_w"], f).reshape(-1)
    vecs[:, V_GNB] = np.asarray(inputs["L0_attn_gn_b"], f).reshape(-1)
    vecs[:, V_NW] = np.asarray(inputs["L1_ffn_norm_w"], f).reshape(-1)
    vecs[:, V_EPSLN] = EPS_LN
    vecs[:, V_EPSGN] = EPS_GN
    vecs[:, V_EPSRMS] = EPS_RMS

    wqkv = np.stack([t(inputs["L1_attn_wq"]), t(inputs["L1_attn_wk"]), t(inputs["L1_attn_wv"])])
    wo = t(inputs["L1_attn_wo"])
    wrt = np.zeros((D, 12), f)
    wm = np.asarray(inputs["L1_ffn_w_macro"], f)
    wu = np.asarray(inputs["L1_ffn_w_micro"], f)
    for gidx in range(4):
        wrt[:, gidx] = wm[gidx]
        for j in range(2):
            wrt[:, 4 + 2 * gidx + j] = wu[gidx, j]
    w1 = np.ascontiguousarray(np.asarray(inputs["L1_ffn_w1"], f).transpose(0, 2, 1))
    w2 = np.ascontiguousarray(np.asarray(inputs["L1_ffn_w2"], f).transpose(0, 2, 1))

    consts = {
        "idn": np.eye(128, dtype=f),
        "gmask": np.zeros((128, 2), f),
        "gsel": np.zeros((2, 128), f),
        "oneh": np.zeros((8, 8 * 128), f),
        "onesk": np.ones((128, 1), f),
        "onebc": np.ones((1, 128), f),
    }
    consts["gmask"][:64, 0] = 1.0
    consts["gmask"][64:, 1] = 1.0
    consts["gsel"][0, :64] = 1.0
    consts["gsel"][1, 64:] = 1.0
    for e in range(8):
        consts["oneh"][e, e * 128:(e + 1) * 128] = 1.0

    shared = dict(consts)
    shared.update(vecs=vecs, wp=wp, wout=t(inputs["L0_attn_W_out"]), wfv=t(inputs["L0_ffn_W_v"]),
                  wqkv=wqkv, wo=wo, wrt=wrt, w1=w1, w2=w2)
    return shared


L0_KEYS = ("vecs", "idn", "gmask", "gsel", "oneh", "onesk", "onebc", "wp", "wout", "wfv")
L1_KEYS = ("vecs", "idn", "gmask", "gsel", "oneh", "onesk", "onebc",
           "wqkv", "wo", "wrt", "w1", "w2")

MODE = "fused"


def kernel(**inputs):
    shared = _host_prep(inputs)
    x = np.ascontiguousarray(np.asarray(inputs["L0_attn_x"], np.float32))
    ncores = 8
    if MODE == "fused":
        nc = build("all")
        maps = []
        for b in range(ncores):
            m = {k: shared[k] for k in set(L0_KEYS + L1_KEYS)}
            m["x"] = x[b]
            maps.append(m)
        res = run_bass_kernel_spmd(nc, maps, core_ids=list(range(ncores)))
        return np.stack([res.results[b]["out"] for b in range(ncores)])
    else:
        ncA = build("l0")
        mapsA = []
        for b in range(ncores):
            m = {k: shared[k] for k in L0_KEYS}
            m["x"] = x[b]
            mapsA.append(m)
        resA = run_bass_kernel_spmd(ncA, mapsA, core_ids=list(range(ncores)))
        ncB = build("l1")
        mapsB = []
        for b in range(ncores):
            m = {k: shared[k] for k in L1_KEYS}
            m["x1"] = np.ascontiguousarray(resA.results[b]["x1"])
            mapsB.append(m)
        resB = run_bass_kernel_spmd(ncB, mapsB, core_ids=list(range(ncores)))
        return np.stack([resB.results[b]["out"] for b in range(ncores)])
